# version 09
# baseline (speedup 1.0000x reference)
import numpy as np
import ml_dtypes

import concourse.bass as bass
from concourse.bass import IndirectOffsetOnAxis
import concourse.mybir as mybir
from concourse.bass_utils import run_bass_kernel_spmd

F32 = mybir.dt.float32
BF16 = mybir.dt.bfloat16
AF = mybir.ActivationFunctionType
ALU = mybir.AluOpType
AX = mybir.AxisListType
I32 = mybir.dt.int32

ENGINES = ("sync", "tensor", "vector", "scalar", "gpsimd")

D = 1024
DC = 8
NH = 16
DH = 64
E = 8
F = 4096
FC = 32
EPS = 1e-5


class KB:

    def __init__(self, nc, blk):
        self.nc = nc
        self.blk = blk
        self.ops = {e: [] for e in ENGINES}
        self.semh = {}
        self.semc = {}
        self.last_w = {}
        self.waited = {e: {} for e in ENGINES}
        self.sem_ctxs = []
        self.retire_gen = 0
        self.retire_keys = []
        self.retire_seen = {e: 0 for e in ENGINES}

    def sem(self, name):
        ctx = self.nc.semaphore(name)
        self.semh[name] = ctx.__enter__()
        self.sem_ctxs.append(ctx)
        self.semc[name] = 0
        return name

    def dsem(self, slot):
        name = "d_" + slot
        if name not in self.semh:
            self.sem(name)
        return name

    def op(self, eng, make, reads=(), writes=(), inc=None, n=1):
        reads = list(reads)
        if self.retire_seen[eng] < self.retire_gen:
            reads += self.retire_keys
            self.retire_seen[eng] = self.retire_gen
        wmax = {}
        for k in reads:
            lw = self.last_w.get(k)
            if lw is not None:
                s, c = lw
                if c > 0 and self.waited[eng].get(s, 0) < c and wmax.get(s, 0) < c:
                    wmax[s] = c
        waits = []
        for s, c in wmax.items():
            waits.append((s, c))
            self.waited[eng][s] = c
        stamp = None
        if inc is not None:
            self.semc[inc] += n
            stamp = self.semc[inc]
            for k in writes:
                self.last_w[k] = (inc, stamp)
        else:
            assert not writes, "tracked writes require an inc semaphore"
        self.ops[eng].append((tuple(waits), make, inc, n))
        return stamp

    def mark(self, key, sem):
        self.last_w[key] = (sem, self.semc[sem])

    def retire(self):
        self.retire_gen += 1
        g = self.retire_gen
        self.retire_keys = []
        for s in self.semc:
            k = ("ret", g, s)
            self.last_w[k] = (s, self.semc[s])
            self.retire_keys.append(k)

    def flush(self):
        for e in ENGINES:
            ops = self.ops[e]
            if not ops:
                continue
            self.ops[e] = []

            def body(engine, ops=ops):
                h = self.semh
                for waits, make, inc, n in ops:
                    for s, c in waits:
                        engine.wait_ge(h[s], c)
                    ins = make(engine)
                    if inc is not None:
                        ins.then_inc(h[inc], n)

            getattr(self.blk, e)(body)


class Scope:
    def __init__(self, nc):
        self.nc = nc
        self.ctxs = []

    def sb(self, name, shape, dt):
        ctx = self.nc.sbuf_tensor(name, shape, dt)
        t = ctx.__enter__()
        self.ctxs.append(ctx)
        return t

    def close(self):
        for c in reversed(self.ctxs):
            c.__exit__(None, None, None)
        self.ctxs = []


def build_program(SEQ, T, debug=False, debug2=False, expose=(), race_detect=True):
    TOK = SEQ * T
    assert T % 512 == 0
    NT1 = TOK // 128
    NT2 = TOK // 512
    TT6 = 512
    NT6 = TOK // TT6
    NTC6 = TT6 // 128
    KC = T // 128
    NW = T // 512

    nc = bass.Bass("TRN2", target_bir_lowering=False,
                   detect_race_conditions=race_detect)

    def din(name, shape, dt=F32):
        return nc.dram_tensor(name, shape, dt, kind="ExternalInput").ap()

    x_in = din("x", [TOK, D])
    Wz_in = din("Wz", [D, D]); bz_in = din("bz", [D])
    Wh_in = din("Wh", [D, D]); bh_in = din("bh", [D])
    g1_in = din("ln1_g", [D]); be1_in = din("ln1_b", [D])
    g2_in = din("ln2_g", [D]); be2_in = din("ln2_b", [D])
    wq_in = din("wq", [D, D]); bq_in = din("bq", [D])
    wk_in = din("wk", [D, D]); bk_in = din("bk", [D])
    wv_in = din("wv", [D, D]); bv_in = din("bv", [D])
    wo_in = din("wo", [D, D]); bo_in = din("bo", [D])
    gw_in = din("gate_w", [D, E]); gb_in = din("gate_b", [E])
    w1_in = din("w1", [E, D, F]); b1_in = din("b1", [E, F])
    w2_in = din("w2", [E, F, D]); b2_in = din("b2", [E, D])
    id_in = din("ident", [128, 128])
    iota_in = din("iotac", [128, 1])

    y_out = nc.dram_tensor("y", [TOK, D], F32, kind="ExternalOutput").ap()

    def scr(name, shape, dt):
        kind = "ExternalOutput" if debug else "Internal"
        return nc.dram_tensor(name, shape, dt, kind=kind).ap()

    xnbo_d = scr("xnbo", [TOK, D], F32)
    xnb_d = scr("xnb", [TOK, D], BF16)
    hnb_d = scr("hnb", [TOK, D], BF16)
    a_d = scr("a_fm", [D, TOK], F32)
    b_d = scr("b_fm", [D, TOK], F32)
    s_d = scr("s_fm", [D, TOK], F32)
    qT_d = scr("qT", [D, TOK], BF16)
    kT_d = scr("kT", [D, TOK], BF16)
    v_d = scr("v_tok", [TOK, D], BF16)
    oT_d = scr("oT", [D, TOK], BF16)
    xp_d = scr("xp", [TOK, D], F32)
    hn2_d = scr("hn2b", [TOK, D], BF16)
    mask_d = scr("mask_c", [TOK // 128, 128, E], F32)
    maskT_d = scr("maskT_c", [E, TOK], F32)
    NTS = TOK // 512 + E - 1
    SLOTS = NTS * 512
    w1s_d = scr("w1s", [E, 128, FC, DC * 128], BF16)
    w2s_d = scr("w2s", [E, 128, FC * D], BF16)
    b1s_d = scr("b1s", [E, 128, FC], F32)
    def scr2(name, shape, dt):
        kind = "ExternalOutput" if (debug or debug2 or name in expose) else "Internal"
        return nc.dram_tensor(name, shape, dt, kind=kind).ap()
    if debug2 or debug:
        pass
    chk_rank = nc.dram_tensor("chk_rank", [TOK], I32, kind="ExternalOutput").ap() \
        if debug or debug2 or ("chk" in expose) else None
    chk_yg = nc.dram_tensor("chk_yg", [128, D], F32, kind="ExternalOutput").ap() \
        if debug or debug2 or ("chk" in expose) else None
    chk_gi = nc.dram_tensor("chk_gi", [128, 1], I32, kind="ExternalOutput").ap() \
        if debug or debug2 or ("chk" in expose) else None
    chk_c = nc.dram_tensor("chk_c", [4, 128, D], F32, kind="ExternalOutput").ap() \
        if debug or debug2 or ("chk" in expose) else None
    hn2s_d = scr2("hn2s", [SLOTS, D], BF16)
    ys_d = scr2("ys", [SLOTS, D], F32)
    rank_d = scr2("rank_d", [TOK], I32)
    if debug:
        dbg_acc = scr("dbg_acc", [TOK, D], F32)
        dbg_mid = scr("dbg_mid", [E, F, TT6], BF16)
        dbg_acce = scr("dbg_acce", [E, TT6, D], F32)

    blk_ctx = nc.Block()
    blk = blk_ctx.__enter__()
    kb = KB(nc, blk)
    qs = kb.sem("qs")
    qg = kb.sem("qg")
    qc = kb.sem("qc")
    ct = kb.sem("ct")
    cv = kb.sem("cv")
    cs = kb.sem("cs")
    cg = kb.sem("cg")

    pers = Scope(nc)
    id_sb = pers.sb("id_sb", [128, 128], F32)
    ones_row = pers.sb("ones_row", [1, 128], F32)
    ones8 = pers.sb("ones8", [128, 8], F32)
    bz_sb = pers.sb("bz_sb", [128, DC], F32)
    bh_sb = pers.sb("bh_sb", [128, DC], F32)
    bq_sb = pers.sb("bq_sb", [128, DC], F32)
    bk_sb = pers.sb("bk_sb", [128, DC], F32)
    ln_tmp = {
        "s1": pers.sb("ln_s1", [128, 4], F32),
        "s2": pers.sb("ln_s2", [128, 4], F32),
        "sq": pers.sb("ln_sq", [128, D], F32),
        "m": pers.sb("ln_m", [128, 4], F32),
        "sd": pers.sb("ln_sd", [128, 4], F32),
        "r": pers.sb("ln_r", [128, 4], F32),
        "nmr": pers.sb("ln_nmr", [128, 4], F32),
    }
    bias_stg = pers.sb("bias_stg", [128, 128], F32)
    ps_all_ctx = nc.psum_tensor("ps_all", [128, 4096], F32)
    ps_all = ps_all_ctx.__enter__()
    ps_ctx = []
    ps = [ps_all[:, i * 512:(i + 1) * 512] for i in range(8)]

    def load_colmajor(dst_ap, src_2d, n):
        assert n <= 128
        kb.op("sync", lambda e: e.dma_start(out=bias_stg[0:n, :], in_=src_2d),
              reads=[("bstg_f",)], writes=[("bstg",)], inc=kb.dsem("bstg"), n=16)
        kb.op("tensor", lambda e: e.matmul(
            ps[7][:, 0:n], bias_stg[0:n, :], id_sb[0:n, 0:n], is_transpose=True),
            reads=[("bstg",), ("id",), ("psf", 7)], writes=[("ps", 7)], inc=ct)
        kb.op("vector", lambda e: e.tensor_copy(dst_ap, ps[7][:, 0:n]),
              reads=[("ps", 7)], writes=[("psf", 7), ("bstg_f",)], inc=cv)

    def bcast_row(scope, nm, src, width=D):
        tile = scope.sb(nm, [128, width], F32)
        row = bc_row_tile[0]
        nhalf = (width + 511) // 512
        kb.op("sync", lambda e, s2=src, w2=width: e.dma_start(
            out=row[:, 0:w2], in_=s2[None, :]),
            reads=[("bcrow_f",)], writes=[("bcrow",)], inc=kb.dsem("bcrow"), n=16)
        for h in range(nhalf):
            w0 = h * 512
            w1 = min(width, w0 + 512)
            kb.op("tensor", lambda e, a=w0, b=w1: e.matmul(
                ps[0][:, 0:b - a], ones_row[:], row[:, a:b], start=True, stop=True),
                reads=[("bcrow",), ("ones_row",), ("psf", 0)],
                writes=[("ps", 0)], inc=ct)
            kb.op("vector", lambda e, a=w0, b=w1, t2=tile: e.tensor_copy(
                t2[:, a:b], ps[0][:, 0:b - a]),
                reads=[("ps", 0)],
                writes=[(nm, h), ("psf", 0), ("bcrow_f",)], inc=cv)
        return tile, [(nm, h) for h in range(nhalf)]

    def emit_ln4(src_flat, src_keys, out_flat, out_key, extra_apply_reads=()):
        s1t, s2t, sq = ln_tmp["s1"], ln_tmp["s2"], ln_tmp["sq"]
        mt, sdt, rt, nmrt = ln_tmp["m"], ln_tmp["sd"], ln_tmp["r"], ln_tmp["nmr"]
        src4 = src_flat.rearrange("p (t d) -> p t d", t=4)
        kb.op("vector", lambda e: e.reduce_sum(s1t[:], src4, axis=AX.X),
              reads=list(src_keys), writes=[("lns1",)], inc=cv)
        for c in range(4):
            kb.op("scalar", lambda e, c2=c: e.activation(
                sq[:], src_flat[:, c2 * D:(c2 + 1) * D], AF.Square,
                accum_out=s2t[:, c2:c2 + 1]),
                reads=list(src_keys) + [("lnsq_f",), ("lnsq",)],
                writes=[("lnsq",)] if c < 3 else [("lnsq",), ("lns2",)], inc=cs)
        kb.mark(("lnsq_f",), cs)
        kb.op("vector", lambda e: e.tensor_single_scalar(mt[:], s1t[:], 1.0 / D, ALU.mult),
              reads=[("lns1",)], writes=[("lnm",)], inc=cv)
        kb.op("vector", lambda e: e.tensor_single_scalar(s2t[:], s2t[:], 1.0 / D, ALU.mult),
              reads=[("lns2",)], writes=[("lnv",)], inc=cv)
        kb.op("vector", lambda e: e.tensor_tensor(
            sdt[:], mt[:], mt[:], ALU.mult),
            reads=[("lnm",)], writes=[("lnmsq",)], inc=cv)
        kb.op("vector", lambda e: e.tensor_tensor(
            sdt[:], s2t[:], sdt[:], ALU.subtract),
            reads=[("lnmsq",), ("lnv",)], writes=[("lnvar",)], inc=cv)
        kb.op("vector", lambda e: e.tensor_single_scalar(sdt[:], sdt[:], EPS, ALU.add),
              reads=[("lnvar",)], writes=[("lnvpe",)], inc=cv)
        kb.op("scalar", lambda e: e.activation(sdt[:], sdt[:], AF.Sqrt),
              reads=[("lnvpe",)], writes=[("lnsd",)], inc=cs)
        kb.op("vector", lambda e: e.reciprocal(rt[:], sdt[:]),
              reads=[("lnsd",)], writes=[("lnr",)], inc=cv)
        kb.op("vector", lambda e: e.scalar_tensor_tensor(
            nmrt[:], mt[:], -1.0, rt[:], ALU.mult, ALU.mult),
            reads=[("lnr",), ("lnm",)], writes=[("lnnmr",)], inc=cv)
        for c in range(4):
            kb.op("scalar", lambda e, c2=c: e.activation(
                out_flat[:, c2 * D:(c2 + 1) * D], src_flat[:, c2 * D:(c2 + 1) * D],
                AF.Identity, bias=nmrt[:, c2:c2 + 1], scale=rt[:, c2:c2 + 1]),
                reads=[("lnnmr",), ("lnr",)] + list(src_keys) + list(extra_apply_reads),
                writes=[out_key, (out_key[0] + "_c", c)], inc=cs)

    kb.op("sync", lambda e: e.dma_start(out=id_sb[:], in_=id_in),
          writes=[("id",)], inc=kb.dsem("idc"), n=16)
    kb.op("vector", lambda e: e.memset(ones_row[:], 1.0), writes=[("ones_row",)], inc=cv)
    kb.op("vector", lambda e: e.memset(ones8[:], 1.0), writes=[("ones8",)], inc=cv)
    for dst, src, key in ((bz_sb, bz_in, "bz"), (bh_sb, bh_in, "bh"),
                          (bq_sb, bq_in, "bq"), (bk_sb, bk_in, "bk")):
        load_colmajor(dst[:], src.rearrange("(c p) -> c p", p=128), DC)
        kb.mark((key,), cv)

    sA = Scope(nc)
    bc_row_tile = [sA.sb("bcrowA", [1, D], F32)]
    g1b, g1b_k = bcast_row(sA, "g1b", g1_in)
    be1b, be1b_k = bcast_row(sA, "be1b", be1_in)
    bvb, bvb_k = bcast_row(sA, "bvb", bv_in)
    bob, bob_k = bcast_row(sA, "bob", bo_in)

    s2w = {}
    for nm, src in (("Wz", Wz_in), ("Wh", Wh_in), ("wq", wq_in),
                    ("wk", wk_in), ("wv", wv_in)):
        s2w[nm] = sA.sb(f"s2w_{nm}", [128, DC * D], BF16)
        kb.op("gpsimd", lambda e, n2=nm, s2=src: e.dma_start(
            out=s2w[n2][:].rearrange("p (k n) -> p k n", k=DC),
            in_=s2.rearrange("(k p) n -> p k n", p=128)),
            writes=[("s2w", nm)], inc=kb.dsem("s2w_" + nm), n=16)

    s1_x = [sA.sb("s1x0", [128, 4 * D], F32)]
    s1_xn = sA.sb("s1xn", [128, 4 * D], F32)
    s1_hn = sA.sb("s1hn", [128, 4 * D], F32)
    s1_xnbo = s1_x[0]

    def emit_s1_macro(i):
        sl = 0
        kb.op("sync", lambda e, i2=i, s2=sl: e.dma_start(
            out=s1_x[s2][:].rearrange("p (t d) -> p t d", t=4),
            in_=x_in[i2 * 512:(i2 + 1) * 512, :].rearrange("(t p) d -> p t d", p=128)),
            reads=[("s1xf", sl)], writes=[("s1x", sl)], inc=kb.dsem("s1x"), n=16)
        extra = [("xnb", i - 1), ("s1xn_ov",)] if i > 0 else []
        emit_ln4(s1_x[sl][:], [("s1x", sl)], s1_xn[:], ("s1xn",),
                 extra_apply_reads=extra)
        for c in range(4):
            kb.op("vector", lambda e, c2=c: e.tensor_add(
                s1_xnbo[:, c2 * D:(c2 + 1) * D], s1_xn[:, c2 * D:(c2 + 1) * D], bob[:]),
                reads=[("s1xn_c", c), ("s1ln2_c", c), ("s1xnbo_f",)] + bob_k,
                writes=[("s1xnbo",), ("s1xnbo_c", c)], inc=cv)
        kb.op("scalar", lambda e, i2=i: e.dma_start(
            out=xnbo_d[i2 * 512:(i2 + 1) * 512, :].rearrange("(t p) d -> p t d", p=128),
            in_=s1_xnbo[:].rearrange("p (t d) -> p t d", t=4)),
            reads=[("s1xnbo",)], writes=[("xnbo", i), ("s1xnbo_f",), ("s1xf", sl)],
            inc=kb.dsem("s1xnbo"), n=16)
        kb.op("gpsimd", lambda e, i2=i: e.dma_start(
            out=xnb_d[i2 * 512:(i2 + 1) * 512, :].rearrange("(t p) d -> p t d", p=128),
            in_=s1_xn[:].rearrange("p (t d) -> p t d", t=4)),
            reads=[("s1xn",)], writes=[("xnb", i)], inc=kb.dsem("s1xnst"), n=16)
        extra_h = [("hnb", i - 1)] if i > 0 else []
        emit_ln4(s1_xn[:], [("s1xn",)], s1_hn[:], ("s1ln2",),
                 extra_apply_reads=extra_h)
        kb.mark(("s1xn_ov",), cs)
        for c in range(4):
            kb.op("vector", lambda e, c2=c: e.tensor_mul(
                s1_hn[:, c2 * D:(c2 + 1) * D], s1_hn[:, c2 * D:(c2 + 1) * D], g1b[:]),
                reads=[("s1ln2_c", c)] + g1b_k, writes=[("s1hg", c)], inc=cv)
            kb.op("vector", lambda e, c2=c: e.tensor_add(
                s1_hn[:, c2 * D:(c2 + 1) * D], s1_hn[:, c2 * D:(c2 + 1) * D], be1b[:]),
                reads=[("s1hg", c)] + be1b_k, writes=[("s1hn",), ("s1hn_c", c)], inc=cv)
        kb.op("gpsimd", lambda e, i2=i: e.dma_start(
            out=hnb_d[i2 * 512:(i2 + 1) * 512, :].rearrange("(t p) d -> p t d", p=128),
            in_=s1_hn[:].rearrange("p (t d) -> p t d", t=4)),
            reads=[("s1hn",)] + [("s1hn_c", c) for c in range(4)],
            writes=[("hnb", i)], inc=kb.dsem("s1hnst"), n=16)

    s2_xnT = [sA.sb(f"s2xnT{i}", [128, DC * 512], BF16) for i in range(2)]
    s2_hnT = [sA.sb(f"s2hnT{i}", [128, DC * 512], BF16) for i in range(2)]
    s2_z = sA.sb("s2z", [128, 512], F32)
    s2_a = sA.sb("s2a", [128, 512], F32)
    s2_ht = sA.sb("s2ht", [128, 512], F32)
    s2_b = sA.sb("s2b", [128, 512], F32)
    s2_qe = sA.sb("s2qe", [128, 512], BF16)
    s2_ke = sA.sb("s2ke", [128, 512], BF16)
    s2_ve = sA.sb("s2ve", [128, 512], BF16)
    bank_ctr = [0]

    def nb():
        b = bank_ctr[0] % 6
        bank_ctr[0] += 1
        return b

    def emit_s2_tile(i):
        sl = i % 2
        t0 = i * 512
        t128 = [t0 // 128 + j for j in range(4)]
        for c in range(DC):
            kb.op("sync", lambda e, i2=i, c2=c, s2=sl: e.dma_start(
                out=s2_xnT[s2][:, c2 * 512:(c2 + 1) * 512],
                in_=xnb_d[i2 * 512:(i2 + 1) * 512, c2 * 128:(c2 + 1) * 128],
                transpose=True),
                reads=[("s2xnT_f", sl), ("xnb", i)],
                writes=[("s2xnT", sl, c)], inc=kb.dsem(f"s2xnT{sl}"), n=16)
            kb.op("sync", lambda e, i2=i, c2=c, s2=sl: e.dma_start(
                out=s2_hnT[s2][:, c2 * 512:(c2 + 1) * 512],
                in_=hnb_d[i2 * 512:(i2 + 1) * 512, c2 * 128:(c2 + 1) * 128],
                transpose=True),
                reads=[("s2hnT_f", sl), ("hnb", i)],
                writes=[("s2hnT", sl, c)], inc=kb.dsem(f"s2hnT{sl}"), n=16)
        xnT_keys = [("s2xnT", sl, c) for c in range(DC)]
        hnT_keys = [("s2hnT", sl, c) for c in range(DC)]

        for m in range(DC):
            bz_ = nb(); bh_ = nb()
            for k in range(DC):
                kb.op("tensor", lambda e, m2=m, k2=k, b2=bz_, s2=sl: e.matmul(
                    ps[b2][:], s2w["Wz"][:, k2 * D + m2 * 128:k2 * D + (m2 + 1) * 128],
                    s2_xnT[s2][:, k2 * 512:(k2 + 1) * 512],
                    start=(k2 == 0), stop=(k2 == DC - 1)),
                    reads=[("s2w", "Wz"), ("psf", bz_)] + xnT_keys,
                    writes=[("ps", bz_)] if k == DC - 1 else [],
                    inc=ct if k == DC - 1 else None)
            for k in range(DC):
                kb.op("tensor", lambda e, m2=m, k2=k, b2=bh_, s2=sl: e.matmul(
                    ps[b2][:], s2w["Wh"][:, k2 * D + m2 * 128:k2 * D + (m2 + 1) * 128],
                    s2_xnT[s2][:, k2 * 512:(k2 + 1) * 512],
                    start=(k2 == 0), stop=(k2 == DC - 1)),
                    reads=[("s2w", "Wh"), ("psf", bh_)] + xnT_keys,
                    writes=[("ps", bh_)] if k == DC - 1 else [],
                    inc=ct if k == DC - 1 else None)
            kb.op("scalar", lambda e, m2=m, b2=bz_: e.activation(
                s2_z[:], ps[b2][:], AF.Sigmoid, bias=bz_sb[:, m2:m2 + 1]),
                reads=[("ps", bz_), ("bz",), ("s2zf",)], writes=[("s2z",)], inc=cs)
            kb.op("scalar", lambda e, m2=m, b2=bh_: e.activation(
                s2_ht[:], ps[b2][:], AF.Identity, bias=bh_sb[:, m2:m2 + 1]),
                reads=[("ps", bh_), ("bh",), ("s2htf",)], writes=[("s2ht",)], inc=cs)
            kb.op("vector", lambda e: e.tensor_scalar(
                s2_a[:], s2_z[:], -1.0, 1.0, ALU.mult, ALU.add),
                reads=[("s2z",), ("s2af",)], writes=[("s2a",), ("psf", bz_)], inc=cv)
            kb.op("vector", lambda e: e.tensor_mul(s2_b[:], s2_z[:], s2_ht[:]),
                reads=[("s2z",), ("s2ht",), ("s2bf",)],
                writes=[("s2b",), ("s2zf",), ("s2htf",), ("psf", bh_)], inc=cv)
            kb.op("scalar", lambda e, m2=m, i2=i: e.dma_start(
                out=a_d[m2 * 128:(m2 + 1) * 128, i2 * 512:(i2 + 1) * 512], in_=s2_a[:]),
                reads=[("s2a",)], writes=[("a_d", m, i), ("s2af",)], inc=kb.dsem("s2ast"), n=16)
            kb.op("scalar", lambda e, m2=m, i2=i: e.dma_start(
                out=b_d[m2 * 128:(m2 + 1) * 128, i2 * 512:(i2 + 1) * 512], in_=s2_b[:]),
                reads=[("s2b",)], writes=[("b_d", m, i), ("s2bf",)], inc=kb.dsem("s2bst"), n=16)

        for wname, bsb, bkey, esb, ekey, dscr, dkey in (
                ("wq", bq_sb, "bq", s2_qe, "s2qe", qT_d, "qT"),
                ("wk", bk_sb, "bk", s2_ke, "s2ke", kT_d, "kT")):
            for m in range(DC):
                bq_ = nb()
                for k in range(DC):
                    kb.op("tensor", lambda e, w2=wname, m2=m, k2=k, b2=bq_, s2=sl: e.matmul(
                        ps[b2][:], s2w[w2][:, k2 * D + m2 * 128:k2 * D + (m2 + 1) * 128],
                        s2_hnT[s2][:, k2 * 512:(k2 + 1) * 512],
                        start=(k2 == 0), stop=(k2 == DC - 1)),
                        reads=[("s2w", wname), ("psf", bq_)] + hnT_keys,
                        writes=[("ps", bq_)] if k == DC - 1 else [],
                        inc=ct if k == DC - 1 else None)
                kb.op("scalar", lambda e, m2=m, b2=bq_, eb=esb, bb=bsb: e.activation(
                    eb[:], ps[b2][:], AF.Identity, bias=bb[:, m2:m2 + 1]),
                    reads=[("ps", bq_), (bkey,), (ekey + "f",)], writes=[(ekey,)], inc=cs)
                kb.op("sync", lambda e, m2=m, i2=i, eb=esb, dd=dscr: e.dma_start(
                    out=dd[m2 * 128:(m2 + 1) * 128, i2 * 512:(i2 + 1) * 512], in_=eb[:]),
                    reads=[(ekey,)], writes=[(dkey, m, i), (ekey + "f",), ("psf", bq_)],
                    inc=kb.dsem(ekey + "st"), n=16)

        for tc in range(4):
            for nhf in range(2):
                bv_ = nb()
                for k in range(DC):
                    kb.op("tensor", lambda e, t2=tc, n2=nhf, k2=k, b2=bv_, s2=sl: e.matmul(
                        ps[b2][:],
                        s2_hnT[s2][:, k2 * 512 + t2 * 128:k2 * 512 + (t2 + 1) * 128],
                        s2w["wv"][:, k2 * D + n2 * 512:k2 * D + (n2 + 1) * 512],
                        start=(k2 == 0), stop=(k2 == DC - 1)),
                        reads=[("s2w", "wv"), ("psf", bv_)] + hnT_keys,
                        writes=[("ps", bv_)] if k == DC - 1 else [],
                        inc=ct if k == DC - 1 else None)
                kb.op("vector", lambda e, n2=nhf, b2=bv_: e.tensor_add(
                    s2_ve[:], ps[b2][:], bvb[:, n2 * 512:(n2 + 1) * 512]),
                    reads=[("ps", bv_), ("s2vef",)] + bvb_k,
                    writes=[("s2ve",)], inc=cv)
                kb.op("gpsimd", lambda e, t2=tc, n2=nhf, i2=i: e.dma_start(
                    out=v_d[i2 * 512 + t2 * 128:i2 * 512 + (t2 + 1) * 128,
                            n2 * 512:(n2 + 1) * 512],
                    in_=s2_ve[:]),
                    reads=[("s2ve",)],
                    writes=[("v_d", i, tc, nhf), ("s2vef",), ("psf", bv_)], inc=kb.dsem("s2vest"), n=16)

        kb.mark(("s2xnT_f", sl), ct)
        kb.mark(("s2hnT_f", sl), ct)

    emit_s1_macro(0)
    for _i2 in range(NT2):
        if _i2 + 1 < NT2:
            emit_s1_macro(_i2 + 1)
        emit_s2_tile(_i2)

    kb.flush()
    sA.close()
    kb.retire()

    sB = Scope(nc)
    s3_a = [sB.sb("s3a0", [128, T], F32)]
    s3_b = [sB.sb("s3b0", [128, T], F32)]
    s3_s = [sB.sb("s3s0", [128, T], F32)]
    s2ps = T // 512
    it = 0
    for sq_ in range(SEQ):
        for m in range(DC):
            sl = 0
            it += 1
            i_range = [sq_ * s2ps + j for j in range(s2ps)]
            kb.op("scalar", lambda e, s2=sl, m2=m, q2=sq_: e.dma_start(
                out=s3_a[s2][:], in_=a_d[m2 * 128:(m2 + 1) * 128, q2 * T:(q2 + 1) * T]),
                reads=[("s3af", sl)] + [("a_d", m, i3) for i3 in i_range],
                writes=[("s3a", sl)], inc=kb.dsem("s3a"), n=16)
            kb.op("scalar", lambda e, s2=sl, m2=m, q2=sq_: e.dma_start(
                out=s3_b[s2][:], in_=b_d[m2 * 128:(m2 + 1) * 128, q2 * T:(q2 + 1) * T]),
                reads=[("s3bf", sl)] + [("b_d", m, i3) for i3 in i_range],
                writes=[("s3b", sl)], inc=kb.dsem("s3b"), n=16)
            kb.op("vector", lambda e, s2=sl: e.tensor_tensor_scan(
                s3_s[s2][:], s3_a[s2][:], s3_b[s2][:], 0.0, ALU.mult, ALU.add),
                reads=[("s3a", sl), ("s3b", sl), ("s3sf", sl)],
                writes=[("s3s", sl), ("s3af", sl), ("s3bf", sl)], inc=cv)
            kb.op("gpsimd", lambda e, s2=sl, m2=m, q2=sq_: e.dma_start(
                out=s_d[m2 * 128:(m2 + 1) * 128, q2 * T:(q2 + 1) * T], in_=s3_s[s2][:]),
                reads=[("s3s", sl)], writes=[("s_d", m, sq_), ("s3sf", sl)], inc=kb.dsem("s3sst"), n=16)


    s0_tmpb = [sB.sb(f"s0b{i}", [128, 2048], BF16) for i in range(2)]
    w1r = w1_in.rearrange("e (k p) (m c) -> e m p k c", p=128, c=128)
    s0_jobs = [("w1", e_, m) for e_ in range(E) for m in range(FC)] + \
              [("w2", e_, k) for e_ in range(E) for k in range(0, FC, 2)]
    for it0, (kind, e_, m) in enumerate(s0_jobs):
        sl = it0 % 2
        if kind == "w1":
            kb.op("gpsimd", lambda en, e2=e_, m2=m, s2=sl: en.dma_start(
                out=s0_tmpb[s2][:, 0:1024].rearrange("p (k c) -> p k c", k=DC),
                in_=w1r[e2, m2]),
                reads=[("s0cf", sl)], writes=[("s0c", sl)], inc=kb.dsem(f"s0l{sl}"), n=16)
            kb.op("gpsimd", lambda en, e2=e_, m2=m, s2=sl: en.dma_start(
                out=w1s_d[e2, :, m2, :], in_=s0_tmpb[s2][:, 0:1024]),
                reads=[("s0c", sl)], writes=[("w1s", e_, m), ("s0cf", sl)], inc=kb.dsem(f"s0s{sl}"), n=16)
        else:
            kb.op("gpsimd", lambda en, e2=e_, k2=m, s2=sl: en.dma_start(
                out=s0_tmpb[s2][:].rearrange("p (j n) -> p j n", j=2),
                in_=w2_in[e2, k2 * 128:(k2 + 2) * 128, :]
                    .rearrange("(j p) n -> p j n", p=128)),
                reads=[("s0cf", sl)], writes=[("s0c", sl)], inc=kb.dsem(f"s0l{sl}"), n=16)
            kb.op("gpsimd", lambda en, e2=e_, k2=m, s2=sl: en.dma_start(
                out=w2s_d[e2, :, k2 * 1024:(k2 + 2) * 1024],
                in_=s0_tmpb[s2][:]),
                reads=[("s0c", sl)],
                writes=[("w2s", e_, m), ("w2s", e_, m + 1), ("s0cf", sl)], inc=kb.dsem(f"s0s{sl}"), n=16)

    kb.mark(("w1s_done", 0), kb.dsem("s0s0"))
    kb.mark(("w1s_done", 1), kb.dsem("s0s1"))

    s4_qf = sB.sb("s4qf", [128, DC * T], BF16)
    s4_kf = sB.sb("s4kf", [128, DC * T], BF16)
    s4_vaug = sB.sb("s4vaug", [128, KC * NH * 65], BF16)
    s4_vstg = sB.sb("s4vstg", [128, 4 * D], BF16)
    s4_probs = sB.sb("s4probs", [128, 2 * KC * 512], BF16)
    s4_zrow = [sB.sb(f"s4zrow{i}", [1, 512], F32) for i in range(2)]
    s4_rz = [sB.sb(f"s4rz{i}", [1, 512], F32) for i in range(2)]
    s4_of = sB.sb("s4of", [64, 512], F32)
    s4_ob = sB.sb("s4ob", [64, 512], BF16)
    kb.op("vector", lambda e: e.memset(
        s4_vaug[:].rearrange("p (kh c) -> p kh c", c=65)[:, :, 64:65], 1.0),
        writes=[("vaug_ones",)], inc=cv)
    for sq_ in range(SEQ):
        i_range = [sq_ * s2ps + j for j in range(s2ps)]
        kb.op("sync", lambda e, q2=sq_: e.dma_start(
            out=s4_qf[:].rearrange("p (c t) -> p c t", c=DC),
            in_=qT_d[:, q2 * T:(q2 + 1) * T].rearrange("(c p) t -> p c t", p=128)),
            reads=[("s4qf_f",)] + [("qT", m, i3) for m in range(DC) for i3 in i_range],
            writes=[("s4qf",)], inc=kb.dsem("s4qf"), n=16)
        kb.op("sync", lambda e, q2=sq_: e.dma_start(
            out=s4_kf[:].rearrange("p (c t) -> p c t", c=DC),
            in_=kT_d[:, q2 * T:(q2 + 1) * T].rearrange("(c p) t -> p c t", p=128)),
            reads=[("s4kf_f",)] + [("kT", m, i3) for m in range(DC) for i3 in i_range],
            writes=[("s4kf",)], inc=kb.dsem("s4kf"), n=16)
        for vi in range(KC // 4):
            v_keys = [("v_d", i3, tc, nhf) for i3 in i_range for tc in range(4)
                      for nhf in range(2)]
            kb.op("sync", lambda e, q2=sq_, v2=vi: e.dma_start(
                out=s4_vstg[:].rearrange("p (c d) -> p c d", c=4),
                in_=v_d[q2 * T + v2 * 512:q2 * T + (v2 + 1) * 512, :]
                    .rearrange("(c p) d -> p c d", p=128)),
                reads=[("s4vstg_f",)] + v_keys, writes=[("s4vstg",)], inc=kb.dsem("s4vstg"), n=16)
            kb.op("vector", lambda e, v2=vi: e.tensor_copy(
                s4_vaug[:, v2 * 4 * NH * 65:(v2 + 1) * 4 * NH * 65]
                    .rearrange("p (k h c) -> p k h c", k=4, h=NH)[:, :, :, 0:64],
                s4_vstg[:].rearrange("p (k h d) -> p k h d", k=4, h=NH)),
                reads=[("s4vstg",), ("vaug_ones",), ("s4vaug_f", vi)],
                writes=[("s4vaug", vi), ("s4vstg_f",)], inc=cv)
        vaug_keys = [("s4vaug", vi) for vi in range(KC // 4)]

        def score_one(cur, kc):
            h, w, par = cur
            pb = 64 * (h % 2)
            chh = h // 2
            bsc = kc % 4
            kb.op("tensor", lambda e, kc2=kc, b2=bsc, pb2=pb, c2=chh, w2=w: e.matmul(
                ps[b2][:],
                s4_kf[pb2:pb2 + 64, c2 * T + kc2 * 128:c2 * T + (kc2 + 1) * 128],
                s4_qf[pb2:pb2 + 64, c2 * T + w2 * 512:c2 * T + (w2 + 1) * 512],
                start=True, stop=True),
                reads=[("s4qf",), ("s4kf",), ("psf", bsc)],
                writes=[("ps", bsc)], inc=ct)
            if kc % 2 == 1:
                b0 = (kc - 1) % 4
                kb.op("scalar", lambda e, kc2=kc, b2=b0, p2=par: e.activation(
                    s4_probs[:, (p2 * KC + kc2 - 1) * 512:(p2 * KC + kc2 + 1) * 512],
                    ps_all[:, b2 * 512:(b2 + 2) * 512], AF.Exp, scale=0.125),
                    reads=[("ps", b0), ("ps", b0 + 1),
                           ("probs_f", par, kc - 1), ("probs_f", par, kc)],
                    writes=[("probs", par, kc - 1), ("probs", par, kc),
                            ("psf", b0), ("psf", b0 + 1)], inc=cs)

        def attnv_one(p, kc):
            h, w, par = p
            ow = 4 + par
            last = (kc == KC - 1)
            kb.op("tensor", lambda e, h2=h, kc2=kc, o2=ow, p2=par: e.matmul(
                ps[o2][0:65, :],
                s4_vaug[:, kc2 * NH * 65 + h2 * 65:kc2 * NH * 65 + (h2 + 1) * 65],
                s4_probs[:, (p2 * KC + kc2) * 512:(p2 * KC + kc2 + 1) * 512],
                start=(kc2 == 0), stop=last),
                reads=[("probs", par, kc), ("psf", ow)] + vaug_keys,
                writes=[("ps", ow)] if last else [],
                inc=ct if last else None)

        def norm_a(p):
            h, w, par = p
            ow = 4 + par
            kb.op("vector", lambda e, o2=ow, p2=par: e.tensor_copy(
                s4_zrow[p2][:], ps[o2][64:65, :]),
                reads=[("ps", ow), ("zrow_f", par)], writes=[("zrow", par)], inc=cv)
            kb.op("vector", lambda e, p2=par: e.reciprocal(s4_rz[p2][:], s4_zrow[p2][:]),
                  reads=[("zrow", par), ("rz_f", par)],
                  writes=[("rz", par), ("zrow_f", par)], inc=cv)

        def norm_b(p, sq2):
            h, w, par = p
            ow = 4 + par
            orz = 6 + par
            kb.op("tensor", lambda e, o2=orz, p2=par: e.matmul(
                ps[o2][0:64, :], ones_row[:, 0:64], s4_rz[p2][:], start=True, stop=True),
                reads=[("rz", par), ("ones_row",), ("psf", orz)],
                writes=[("ps", orz), ("rz_f", par)], inc=ct)
            kb.op("vector", lambda e, o2=ow: e.tensor_copy(s4_of[:], ps[o2][0:64, :]),
                  reads=[("ps", ow), ("s4of_f",)], writes=[("s4of",)], inc=cv)
            kb.op("vector", lambda e, o2=orz: e.tensor_mul(
                s4_ob[:], s4_of[:], ps[o2][0:64, :]),
                reads=[("s4of",), ("ps", orz), ("s4ob_f",)],
                writes=[("s4ob",), ("psf", ow), ("psf", orz), ("s4of_f",)], inc=cv)
            kb.op("sync", lambda e, h2=h, q2=sq2, w2=w: e.dma_start(
                out=oT_d[h2 * 64:(h2 + 1) * 64,
                         q2 * T + w2 * 512:q2 * T + (w2 + 1) * 512],
                in_=s4_ob[:]),
                reads=[("s4ob",)], writes=[("oT", sq2, w, h), ("s4ob_f",)],
                inc=kb.dsem("s4obst"), n=16)

        pend1 = pend2 = None
        for idx, (h, w) in enumerate([(h, w) for h in range(NH) for w in range(NW)]):
            cur = (h, w, idx % 2)
            for kc in range(KC):
                score_one(cur, kc)
                if pend1 is not None:
                    attnv_one(pend1, kc)
            if pend1 is not None:
                for kc in range(KC):
                    kb.mark(("probs_f", pend1[2], kc), ct)
            if pend2 is not None:
                norm_b(pend2, sq_)
            if pend1 is not None:
                norm_a(pend1)
            pend2, pend1 = pend1, cur
        for kc in range(KC):
            attnv_one(pend1, kc)
        for kc in range(KC):
            kb.mark(("probs_f", pend1[2], kc), ct)
        norm_b(pend2, sq_)
        norm_a(pend1)
        norm_b(pend1, sq_)
        kb.mark(("s4qf_f",), ct)
        kb.mark(("s4kf_f",), ct)
        for vi in range(KC // 4):
            kb.mark(("s4vaug_f", vi), ct)

    kb.flush()
    sB.close()
    kb.retire()

    sC = Scope(nc)
    bc_row_tile[0] = sC.sb("bcrowC", [1, D], F32)
    g2b, g2b_k = bcast_row(sC, "g2b", g2_in)
    be2b, be2b_k = bcast_row(sC, "be2b", be2_in)
    gw_sb = sC.sb("gw_sb", [128, DC * E], F32)
    gbb_sb = sC.sb("gbb_sb", [128, E], F32)
    kb.op("sync", lambda e: e.dma_start(
        out=gw_sb[:].rearrange("p (c e) -> p c e", e=E),
        in_=gw_in.rearrange("(c p) e -> p c e", p=128)),
        writes=[("gw",)], inc=kb.dsem("gw"), n=16)
    kb.op("sync", lambda e: e.dma_start(out=bc_row_tile[0][:, 0:E], in_=gb_in[None, :]),
          reads=[("bcrow_f",)], writes=[("bcrowg",)], inc=kb.dsem("bcrow"), n=16)
    kb.op("tensor", lambda e: e.matmul(
        ps[0][:, 0:E], ones_row[:], bc_row_tile[0][:, 0:E], start=True, stop=True),
        reads=[("bcrowg",), ("ones_row",), ("psf", 0)], writes=[("ps", 0)], inc=ct)
    kb.op("vector", lambda e: e.tensor_copy(gbb_sb[:], ps[0][:, 0:E]),
          reads=[("ps", 0)], writes=[("gbb",), ("psf", 0)], inc=cv)

    s5_wo = sC.sb("s5wo", [128, DC * D], BF16)
    kb.op("gpsimd", lambda e: e.dma_start(
        out=s5_wo[:].rearrange("p (k n) -> p k n", k=DC),
        in_=wo_in.rearrange("(k p) n -> p k n", p=128)),
        writes=[("s5wo",)], inc=kb.dsem("s5wo"), n=16)

    s5_oT = [sC.sb(f"s5oT{i}", [128, DC * 512], BF16) for i in range(2)]
    s5_xnbo = [sC.sb(f"s5xnbo{i}", [128, 4 * D], F32) for i in range(2)]
    s5_xp = sC.sb("s5xp", [128, 4 * D], F32)
    s5_hn2 = sC.sb("s5hn2", [128, 4 * D], F32)
    s5_hn2T = sC.sb("s5hn2T", [128, DC * 128], F32)
    s5_lg = sC.sb("s5lg", [128, E], F32)
    s5_mx = sC.sb("s5mx", [128, 1], F32)
    s5_raw = sC.sb("s5raw", [128, E], F32)
    s5_cnt = sC.sb("s5cnt", [128, E], F32)
    s5_msk = sC.sb("s5msk", [128, E], F32)
    s5_mT = sC.sb("s5mT", [8, 128], F32)
    prev_tile = [None]
    for i in range(NT2):
        sl = i % 2
        t0 = i * 512
        sq_of = t0 // T
        w_of = (t0 % T) // 512
        kb.op("sync", lambda e, i2=i, s2=sl: e.dma_start(
            out=s5_oT[s2][:].rearrange("p (c t) -> p c t", c=DC),
            in_=oT_d[:, i2 * 512:(i2 + 1) * 512].rearrange("(c p) t -> p c t", p=128)),
            reads=[("s5oT_f", sl)] + [("oT", sq_of, w_of, h) for h in range(NH)],
            writes=[("s5oT", sl)], inc=kb.dsem(f"s5oT{sl}"), n=16)
        kb.op("sync", lambda e, i2=i, s2=sl: e.dma_start(
            out=s5_xnbo[s2][:].rearrange("p (c d) -> p c d", c=4),
            in_=xnbo_d[i2 * 512:(i2 + 1) * 512, :].rearrange("(c p) d -> p c d", p=128)),
            reads=[("s5xnbo_f", sl), ("xnbo", i)],
            writes=[("s5xnbo", sl)], inc=kb.dsem(f"s5xnbo{sl}"), n=16)

        for tc in range(4):
            for nhf in range(2):
                bb = nhf
                for k in range(DC):
                    kb.op("tensor", lambda e, t2=tc, n2=nhf, k2=k, b2=bb, s2=sl: e.matmul(
                        ps[b2][:],
                        s5_oT[s2][:, k2 * 512 + t2 * 128:k2 * 512 + (t2 + 1) * 128],
                        s5_wo[:, k2 * D + n2 * 512:k2 * D + (n2 + 1) * 512],
                        start=(k2 == 0), stop=(k2 == DC - 1)),
                        reads=[("s5wo",), ("s5oT", sl), ("psf", bb)],
                        writes=[("ps", bb)] if k == DC - 1 else [],
                        inc=ct if k == DC - 1 else None)
                extra = [("xp", prev_tile[0])] if prev_tile[0] is not None else []
                kb.op("vector", lambda e, t2=tc, n2=nhf, b2=bb, s2=sl: e.tensor_add(
                    s5_xp[:, t2 * D + n2 * 512:t2 * D + (n2 + 1) * 512], ps[b2][:],
                    s5_xnbo[s2][:, t2 * D + n2 * 512:t2 * D + (n2 + 1) * 512]),
                    reads=[("ps", bb), ("s5xnbo", sl), ("s5xp_f",)] + extra,
                    writes=[("s5xp",), ("s5xph", tc, nhf), ("psf", bb)], inc=cv)
        kb.op("gpsimd", lambda e, i2=i: e.dma_start(
            out=xp_d[i2 * 512:(i2 + 1) * 512, :].rearrange("(t p) d -> p t d", p=128),
            in_=s5_xp[:].rearrange("p (t d) -> p t d", t=4)),
            reads=[("s5xph", t2, n2) for t2 in range(4) for n2 in range(2)],
            writes=[("xp", i)], inc=kb.dsem("s5xpst"), n=16)
        extra_h = [("hn2b", prev_tile[0])] if prev_tile[0] is not None else []
        emit_ln4(s5_xp[:], [("s5xph", t2, n2) for t2 in range(4) for n2 in range(2)],
                 s5_hn2[:], ("s5ln",), extra_apply_reads=extra_h + [("s5hn2_f",)])
        kb.mark(("s5xp_f",), cs)
        for c in range(4):
            kb.op("vector", lambda e, c2=c: e.tensor_mul(
                s5_hn2[:, c2 * D:(c2 + 1) * D], s5_hn2[:, c2 * D:(c2 + 1) * D], g2b[:]),
                reads=[("s5ln_c", c)] + g2b_k, writes=[("s5hg", c)], inc=cv)
            kb.op("vector", lambda e, c2=c: e.tensor_add(
                s5_hn2[:, c2 * D:(c2 + 1) * D], s5_hn2[:, c2 * D:(c2 + 1) * D], be2b[:]),
                reads=[("s5hg", c)] + be2b_k,
                writes=[("s5hn2",), ("s5hn2_c", c)], inc=cv)
        kb.op("gpsimd", lambda e, i2=i: e.dma_start(
            out=hn2_d[i2 * 512:(i2 + 1) * 512, :].rearrange("(t p) d -> p t d", p=128),
            in_=s5_hn2[:].rearrange("p (t d) -> p t d", t=4)),
            reads=[("s5hn2",)] + [("s5hn2_c", c) for c in range(4)],
            writes=[("hn2b", i)], inc=kb.dsem("s5hn2st"), n=16)

        for tc in range(4):
            chunk = t0 // 128 + tc
            for c in range(DC):
                tb = 2 + c // 4
                last4 = (c % 4 == 3)
                kb.op("tensor", lambda e, t2=tc, c2=c, b2=tb: e.matmul(
                    ps[b2][:, (c2 % 4) * 128:(c2 % 4 + 1) * 128],
                    s5_hn2[:, t2 * D + c2 * 128:t2 * D + (c2 + 1) * 128],
                    id_sb[:], is_transpose=True),
                    reads=[("s5hn2_c", tc), ("id",), ("psf", tb)],
                    writes=[("ps", tb)] if last4 else [],
                    inc=ct if last4 else None)
            for tb in (2, 3):
                kb.op("vector", lambda e, b2=tb: e.tensor_copy(
                    s5_hn2T[:, (b2 - 2) * 512:(b2 - 1) * 512], ps[b2][:]),
                    reads=[("ps", tb), ("hn2T_f",)],
                    writes=[("hn2T", tb), ("psf", tb)], inc=cv)
            for k in range(DC):
                kb.op("tensor", lambda e, k2=k: e.matmul(
                    ps[4][:, 0:E], s5_hn2T[:, k2 * 128:(k2 + 1) * 128],
                    gw_sb[:].rearrange("p (c e) -> p c e", e=E)[:, k2, :],
                    start=(k2 == 0), stop=(k2 == DC - 1)),
                    reads=[("hn2T", 2), ("hn2T", 3), ("gw",), ("psf", 4)],
                    writes=[("ps", 4)] if k == DC - 1 else [],
                    inc=ct if k == DC - 1 else None)
            kb.mark(("hn2T_f",), ct)
            kb.op("vector", lambda e: e.tensor_add(s5_lg[:], ps[4][:, 0:E], gbb_sb[:]),
                  reads=[("ps", 4), ("gbb",), ("s5lg_f",)],
                  writes=[("s5lg",), ("psf", 4)], inc=cv)
            kb.op("vector", lambda e: e.reduce_max(s5_mx[:], s5_lg[:], axis=AX.X),
                  reads=[("s5lg",)], writes=[("s5mx",)], inc=cv)
            kb.op("vector", lambda e: e.scalar_tensor_tensor(
                s5_raw[:], s5_lg[:], s5_mx[:], ones8[:], ALU.is_equal, ALU.mult),
                reads=[("s5mx",), ("ones8",)], writes=[("s5raw",), ("s5lg_f",)], inc=cv)
            kb.op("vector", lambda e: e.tensor_tensor_scan(
                s5_cnt[:], ones8[:], s5_raw[:], 0.0, ALU.mult, ALU.add),
                reads=[("s5raw",), ("ones8",)], writes=[("s5cnt",)], inc=cv)
            kb.op("vector", lambda e: e.tensor_single_scalar(
                s5_cnt[:], s5_cnt[:], 1.5, ALU.is_le),
                reads=[("s5cnt",)], writes=[("s5cnt2",)], inc=cv)
            kb.op("vector", lambda e: e.tensor_mul(s5_msk[:], s5_raw[:], s5_cnt[:]),
                  reads=[("s5cnt2",), ("s5raw",), ("s5msk_f",), ("s5msk_f2",)],
                  writes=[("s5msk",)], inc=cv)
            kb.op("gpsimd", lambda e, c2=chunk: e.dma_start(
                out=mask_d[c2], in_=s5_msk[:]),
                reads=[("s5msk",)], writes=[("mask", chunk), ("s5msk_f2",)],
                inc=kb.dsem("s5mskst"), n=16)
            kb.op("tensor", lambda e: e.matmul(
                ps[5][0:8, 0:128], s5_msk[:], id_sb[:], is_transpose=True),
                reads=[("s5msk",), ("id",), ("psf", 5)], writes=[("ps", 5)], inc=ct)
            kb.op("vector", lambda e: e.tensor_copy(s5_mT[:], ps[5][0:8, 0:128]),
                  reads=[("ps", 5), ("s5mT_f",)], writes=[("s5mT",), ("psf", 5)], inc=cv)
            kb.op("gpsimd", lambda e, c2=chunk: e.dma_start(
                out=maskT_d[:, c2 * 128:(c2 + 1) * 128], in_=s5_mT[:]),
                reads=[("s5mT",)], writes=[("maskT", chunk), ("s5mT_f",), ("s5msk_f",)],
                inc=kb.dsem("s5mTst"), n=16)
        prev_tile[0] = i
        kb.mark(("s5hn2_f",), ct)
        kb.mark(("s5oT_f", sl), ct)
        kb.mark(("s5xnbo_f", sl), cv)

    kb.flush()
    sC.close()
    kb.retire()

    _bregs = {}

    def breg(e, v):
        if v not in _bregs:
            _bregs[v] = e.to_reg(v)
        return _bregs[v]

    sD1 = Scope(nc)
    iota_sb = sD1.sb("iota_sb", [128, 1], F32)
    teb = sD1.sb("teb", [128, NTS], F32)
    wxf = sD1.sb("wxf", [128, NTS], F32)
    widx1 = sD1.sb("widx1", [128, NTS], I32)
    widx2 = sD1.sb("widx2", [128, NTS], I32)
    b2e_sb = sD1.sb("b2e_sb", [8, D], F32)
    kb.op("sync", lambda e: e.dma_start(out=iota_sb[:], in_=iota_in),
          writes=[("iotac",)], inc=kb.dsem("iota"), n=16)
    kb.op("sync", lambda e: e.dma_start(out=b2e_sb[:], in_=b2_in),
          writes=[("b2e",)], inc=kb.dsem("b2e"), n=16)
    for e_ in range(E):
        kb.op("sync", lambda e, e2=e_: e.dma_start(
            out=bias_stg[0:32, :], in_=b1_in[e2].rearrange("(m p) -> m p", p=128)),
            reads=[("bstg_f",)], writes=[("bstg",)], inc=kb.dsem("bstg"), n=16)
        kb.op("tensor", lambda e: e.matmul(
            ps[7][:, 0:32], bias_stg[0:32, :], id_sb[0:32, 0:32], is_transpose=True),
            reads=[("bstg",), ("id",), ("psf", 7)], writes=[("ps", 7)], inc=ct)
        kb.op("vector", lambda e: e.tensor_copy(ln_tmp["sq"][:, 0:32], ps[7][:, 0:32]),
              reads=[("ps", 7), ("lnsq_f",)], writes=[("b1row",), ("psf", 7), ("bstg_f",)],
              inc=cv)
        kb.op("gpsimd", lambda e, e2=e_: e.dma_start(
            out=b1s_d[e2], in_=ln_tmp["sq"][:, 0:32]),
            reads=[("b1row",)], writes=[("b1s", e_), ("lnsq_f",)], inc=kb.dsem("b1sst"), n=16)
    kb.mark(("b1s_all",), kb.dsem("b1sst"))

    sR = Scope(nc)
    mT8 = sR.sb("mT8", [8, TOK], F32)
    on8 = sR.sb("on8", [8, TOK], F32)
    cum8 = sR.sb("cum8", [8, TOK], F32)
    rankr = sR.sb("rankr", [1, TOK], F32)
    ranki = sR.sb("ranki", [1, TOK], I32)
    nt8 = sR.sb("nt8", [8, 1], F32)
    nti = sR.sb("nti", [8, 1], I32)
    ntr = sR.sb("ntr", [1, 8], F32)
    cumr = sR.sb("cumr", [1, 8], F32)
    offr = sR.sb("offr", [1, 8], F32)
    offc = sR.sb("offc", [8, 1], F32)
    inclc = sR.sb("inclc", [8, 1], F32)
    ior = sR.sb("ior", [1, NTS], F32)
    i8r = sR.sb("i8r", [8, NTS], F32)
    ger = sR.sb("ger", [8, NTS], F32)
    ter = sR.sb("ter", [1, NTS], F32)
    onn = sR.sb("onn", [128, NTS], F32)

    kb.op("sync", lambda e: e.dma_start(out=mT8[:], in_=maskT_d),
          reads=[("maskT", ch) for ch in range(TOK // 128)],
          writes=[("rt_m",)], inc=kb.dsem("mT8"), n=16)
    kb.op("vector", lambda e: e.memset(on8[:], 1.0), writes=[("rt_on",)], inc=cv)
    kb.op("vector", lambda e: e.memset(onn[:], 1.0), writes=[("rt_onn",)], inc=cv)
    kb.op("vector", lambda e: e.reduce_sum(nt8[:], mT8[:], axis=AX.X),
          reads=[("rt_m",)], writes=[("rt_cnt",)], inc=cv)
    kb.op("vector", lambda e: e.tensor_tensor_scan(
        cum8[:], on8[:], mT8[:], 0.0, ALU.mult, ALU.add),
        reads=[("rt_m",), ("rt_on",)], writes=[("rt_cum",)], inc=cv)
    kb.op("vector", lambda e: e.tensor_scalar(
        nt8[:], nt8[:], 511.0, 1.0 / 512.0, ALU.add, ALU.mult),
        reads=[("rt_cnt",)], writes=[("rt_nt1",)], inc=cv)
    kb.op("vector", lambda e: e.tensor_single_scalar(nt8[:], nt8[:], -0.4995, ALU.add),
          reads=[("rt_nt1",)], writes=[("rt_nt2",)], inc=cv)
    kb.op("vector", lambda e: e.tensor_copy(nti[:], nt8[:]),
          reads=[("rt_nt2",)], writes=[("rt_nti",)], inc=cv)
    kb.op("vector", lambda e: e.tensor_copy(nt8[:], nti[:]),
          reads=[("rt_nti",)], writes=[("rt_ntf",)], inc=cv)
    kb.op("tensor", lambda e: e.matmul(
        ps[7][0:1, 0:8], nt8[:], id_sb[0:8, 0:8], is_transpose=True),
        reads=[("rt_ntf",), ("id",), ("psf", 7)], writes=[("ps", 7)], inc=ct)
    kb.op("vector", lambda e: e.tensor_copy(ntr[:], ps[7][0:1, 0:8]),
          reads=[("ps", 7)], writes=[("rt_ntr",), ("psf", 7)], inc=cv)
    kb.op("vector", lambda e: e.tensor_tensor_scan(
        cumr[:], ones_row[0:1, 0:8], ntr[:], 0.0, ALU.mult, ALU.add),
        reads=[("rt_ntr",), ("ones_row",)], writes=[("rt_cumr",)], inc=cv)
    kb.op("vector", lambda e: e.tensor_sub(offr[:], cumr[:], ntr[:]),
          reads=[("rt_cumr",), ("rt_ntr",)], writes=[("rt_offr",)], inc=cv)
    kb.op("vector", lambda e: e.tensor_scalar(
        offr[:], offr[:], 512.0, -1.0, ALU.mult, ALU.add),
        reads=[("rt_offr",)], writes=[("rt_offm1",)], inc=cv)
    kb.op("tensor", lambda e: e.matmul(
        ps[7][0:8, 0:1], offr[:], id_sb[0:1, 0:1], is_transpose=True),
        reads=[("rt_offm1",), ("id",), ("psf", 7)], writes=[("ps", 7)], inc=ct)
    kb.op("vector", lambda e: e.tensor_copy(offc[:], ps[7][0:8, 0:1]),
          reads=[("ps", 7)], writes=[("rt_offc",), ("psf", 7)], inc=cv)
    kb.op("tensor", lambda e: e.matmul(
        ps[6][0:8, 0:1], cumr[:], id_sb[0:1, 0:1], is_transpose=True),
        reads=[("rt_cumr",), ("id",), ("psf", 6)], writes=[("ps", 6)], inc=ct)
    kb.op("vector", lambda e: e.tensor_copy(inclc[:], ps[6][0:8, 0:1]),
          reads=[("ps", 6)], writes=[("rt_inclc",), ("psf", 6)], inc=cv)
    kb.op("vector", lambda e: e.scalar_tensor_tensor(
        cum8[:], cum8[:], offc[:], mT8[:], ALU.add, ALU.mult),
        reads=[("rt_cum",), ("rt_offc",), ("rt_m",)], writes=[("rt_rk8",)], inc=cv)
    for c in range(TOK // 512):
        bb = 4 + c % 2
        kb.op("tensor", lambda e, c2=c, b2=bb: e.matmul(
            ps[b2][0:1, :], on8[:, 0:1], cum8[:, c2 * 512:(c2 + 1) * 512],
            start=True, stop=True),
            reads=[("rt_rk8",), ("rt_on",), ("psf", bb)], writes=[("ps", bb)], inc=ct)
        kb.op("vector", lambda e, c2=c, b2=bb: e.tensor_copy(
            rankr[:, c2 * 512:(c2 + 1) * 512], ps[b2][0:1, :]),
            reads=[("ps", bb)], writes=[("rt_rkr",), ("psf", bb)], inc=cv)
    kb.op("vector", lambda e: e.tensor_copy(ranki[:], rankr[:]),
          reads=[("rt_rkr",)], writes=[("rt_rki",)], inc=cv)
    kb.op("gpsimd", lambda e: e.dma_start(
        out=rank_d.rearrange("(a t) -> a t", a=1), in_=ranki[:]),
        reads=[("rt_rki",)], writes=[("rankd",)], inc=kb.dsem("rankst"), n=16)
    kb.op("vector", lambda e: e.tensor_tensor_scan(
        ior[:], ones_row[0:1, 0:NTS], ones_row[0:1, 0:NTS], 0.0, ALU.mult, ALU.add),
        reads=[("ones_row",)], writes=[("rt_io1",)], inc=cv)
    kb.op("vector", lambda e: e.tensor_single_scalar(ior[:], ior[:], -1.0, ALU.add),
          reads=[("rt_io1",)], writes=[("rt_io",)], inc=cv)
    kb.op("tensor", lambda e: e.matmul(
        ps[4][0:8, 0:NTS], ones_row[0:1, 0:8], ior[:], start=True, stop=True),
        reads=[("rt_io",), ("ones_row",), ("psf", 4)], writes=[("ps", 4)], inc=ct)
    kb.op("vector", lambda e: e.tensor_copy(i8r[:], ps[4][0:8, 0:NTS]),
          reads=[("ps", 4)], writes=[("rt_i8",), ("psf", 4)], inc=cv)
    kb.op("vector", lambda e: e.scalar_tensor_tensor(
        ger[:], i8r[:], inclc[:], on8[:, 0:NTS], ALU.is_ge, ALU.mult),
        reads=[("rt_i8",), ("rt_inclc",), ("rt_on",)], writes=[("rt_ge",)], inc=cv)
    kb.op("tensor", lambda e: e.matmul(
        ps[5][0:1, 0:NTS], on8[:, 0:1], ger[:], start=True, stop=True),
        reads=[("rt_ge",), ("rt_on",), ("psf", 5)], writes=[("ps", 5)], inc=ct)
    kb.op("vector", lambda e: e.tensor_copy(ter[:], ps[5][0:1, 0:NTS]),
          reads=[("ps", 5)], writes=[("rt_te",), ("psf", 5)], inc=cv)
    kb.op("tensor", lambda e: e.matmul(
        ps[6][:, 0:NTS], ones_row[:], ter[:], start=True, stop=True),
        reads=[("rt_te",), ("ones_row",), ("psf", 6)], writes=[("ps", 6)], inc=ct)
    kb.op("vector", lambda e: e.tensor_copy(teb[:], ps[6][:, 0:NTS]),
          reads=[("ps", 6)], writes=[("rt_teb",), ("psf", 6)], inc=cv)
    kb.op("vector", lambda e: e.tensor_single_scalar(wxf[:], teb[:], 1024.0, ALU.mult),
          reads=[("rt_teb",)], writes=[("rt_w1a",)], inc=cv)
    kb.op("vector", lambda e: e.tensor_single_scalar(
        ln_tmp["sq"][:, 0:1], iota_sb[:], 8.0, ALU.mult),
        reads=[("iotac",), ("lnsq_f",)], writes=[("rt_i32s",)], inc=cv)
    kb.op("vector", lambda e: e.scalar_tensor_tensor(
        wxf[:], wxf[:], ln_tmp["sq"][:, 0:1], onn[:], ALU.add, ALU.mult),
        reads=[("rt_w1a",), ("rt_i32s",), ("rt_onn",)], writes=[("rt_w1b",)], inc=cv)
    kb.op("vector", lambda e: e.tensor_copy(widx1[:], wxf[:]),
          reads=[("rt_w1b",)], writes=[("widx1",)], inc=cv)
    kb.op("vector", lambda e: e.tensor_single_scalar(wxf[:], teb[:], 128.0, ALU.mult),
          reads=[("rt_teb",), ("widx1",)], writes=[("rt_w2a",)], inc=cv)
    kb.op("vector", lambda e: e.scalar_tensor_tensor(
        wxf[:], wxf[:], iota_sb[:], onn[:], ALU.add, ALU.mult),
        reads=[("rt_w2a",), ("iotac",), ("rt_onn",)], writes=[("rt_w2b",)], inc=cv)
    kb.op("vector", lambda e: e.tensor_copy(widx2[:], wxf[:]),
          reads=[("rt_w2b",)], writes=[("widx2",)], inc=cv)
    kb.mark(("lnsq_f",), cv)

    kb.flush()
    sR.close()
    kb.retire()

    sD = Scope(nc)
    sc_h = [sD.sb(f"sc_h{i}", [128, D], BF16) for i in range(2)]
    sc_i = [sD.sb(f"sc_i{i}", [128, 1], I32) for i in range(2)]
    s6_hn2T = [sD.sb(f"s6hn2T{i}", [128, DC * TT6], BF16) for i in range(2)]
    s6_w1 = [sD.sb(f"s6w1_{i}", [128, 4 * DC * 128], BF16) for i in range(2)]
    s6_w2 = sD.sb("s6w2", [128, FC * D], BF16)
    s6_b1 = sD.sb("s6b1", [128, FC], F32)
    s6_mid = sD.sb("s6mid", [128, FC * TT6], BF16)
    s6_ys = [sD.sb(f"s6ys{i}", [128, D], F32) for i in range(2)]
    s6_yg = [sD.sb(f"s6yg{i}", [128, D], F32) for i in range(2)]
    s6_gi = [sD.sb(f"s6gi{i}", [128, 1], I32) for i in range(2)]
    s6_mT = sD.sb("s6mT", [8, TT6], F32)
    s6_xp = [sD.sb(f"s6xp{i}", [128, D], F32) for i in range(2)]
    s6_x = [sD.sb(f"s6x{i}", [128, D], F32) for i in range(2)]
    s6_s = sD.sb("s6s", [128, DC * TT6], F32)
    s6_cmb = sD.sb("s6cmb", [128, D], F32)
    s6_g = sD.sb("s6g", [128, D], F32)
    s6_out = sD.sb("s6out", [128, D], F32)

    for c in range(TOK // 128):
        sl = c % 2
        kb.op("scalar", lambda e, c2=c, s9=sl: e.dma_start(
            out=sc_h[s9][:], in_=hn2_d[c2 * 128:(c2 + 1) * 128, :]),
            reads=[("sc_h_f", sl), ("hn2b", c // 4)], writes=[("sc_h", sl)], inc=kb.dsem(f"sch{sl}"), n=16)
        kb.op("scalar", lambda e, c2=c, s9=sl: e.dma_start(
            out=sc_i[s9][:],
            in_=rank_d.rearrange("(c p) -> c p", p=128)[c2][:, None]),
            reads=[("sc_i_f", sl), ("rankd",)], writes=[("sc_i", sl)], inc=kb.dsem(f"sci{sl}"), n=16)
        kb.op("gpsimd", lambda e, s9=sl: e.indirect_dma_start(
            out=hn2s_d[0:128, :], out_offset=IndirectOffsetOnAxis(sc_i[s9][:], 0),
            in_=sc_h[s9][:], in_offset=None,
            bounds_check=breg(e, SLOTS - 1), oob_is_err=False),
            reads=[("sc_h", sl), ("sc_i", sl)],
            writes=[("hn2s_s", sl), ("sc_h_f", sl), ("sc_i_f", sl)], inc=kb.dsem(f"scat{sl}"), n=16)

    for j in range(NTS):
        st6 = j % 2
        for c in range(DC):
            kb.op("scalar", lambda e, j2=j, c2=c, s9=st6: e.dma_start(
                out=s6_hn2T[s9][:, c2 * TT6:(c2 + 1) * TT6],
                in_=hn2s_d[j2 * 512:(j2 + 1) * 512, c2 * 128:(c2 + 1) * 128],
                transpose=True),
                reads=[("s6hn2T_f", st6), ("hn2s_s", 0), ("hn2s_s", 1)],
                writes=[("s6hn2T", st6, c)], inc=kb.dsem(f"s6hn2T{st6}"), n=16)
        hn2T_keys = [("s6hn2T", st6, c) for c in range(DC)]
        kb.op("gpsimd", lambda e, j2=j: e.indirect_dma_start(
            out=s6_w2[:], out_offset=None,
            in_=w2s_d.rearrange("e p n -> (e p) n")[0:128, :],
            in_offset=IndirectOffsetOnAxis(widx2[:, j2:j2 + 1], 0),
            bounds_check=breg(e, E * 128 - 1), oob_is_err=False),
            reads=[("w1s_done", 0), ("w1s_done", 1), ("widx2",), ("s6w2_f",)],
            writes=[("s6w2",)], inc=kb.dsem("s6w2"), n=16)
        kb.op("gpsimd", lambda e, j2=j: e.indirect_dma_start(
            out=s6_b1[:], out_offset=None,
            in_=b1s_d.rearrange("e p m -> (e p) m")[0:128, :],
            in_offset=IndirectOffsetOnAxis(widx2[:, j2:j2 + 1], 0),
            bounds_check=breg(e, E * 128 - 1), oob_is_err=False),
            reads=[("b1s_all",), ("widx2",), ("s6b1_f",)],
            writes=[("s6b1",)], inc=kb.dsem("s6b1"), n=16)
        for mg in range(FC // 4):
            sl = mg % 2
            kb.op("gpsimd", lambda e, j2=j, m2=mg, s9=sl: e.indirect_dma_start(
                out=s6_w1[s9][:], out_offset=None,
                in_=w1s_d.rearrange("e p m n -> (e p) (m n)")[0:128, 0:4 * 1024],
                in_offset=IndirectOffsetOnAxis(widx1[:, j2:j2 + 1], 0),
                element_offset=m2 * 4 * 1024,
                bounds_check=breg(e, E * 128 * FC // 4 - 1), oob_is_err=False),
                reads=[("w1s_done", 0), ("w1s_done", 1), ("widx1",), ("s6w1_f", sl)],
                writes=[("s6w1", sl)], inc=kb.dsem(f"s6w1_{sl}"), n=16)
            for mm in range(4):
                m = mg * 4 + mm
                bu = m % 2
                for k in range(DC):
                    last = (k == DC - 1)
                    rel = last and (mm == 3)
                    kb.op("tensor", lambda en, m3=mm, k2=k, b2=bu, s9=sl, s8=st6: en.matmul(
                        ps[b2][:],
                        s6_w1[s9][:, m3 * 1024 + k2 * 128:m3 * 1024 + (k2 + 1) * 128],
                        s6_hn2T[s8][:, k2 * TT6:(k2 + 1) * TT6],
                        start=(k2 == 0), stop=last),
                        reads=[("s6w1", sl), ("psf", bu)] + hn2T_keys,
                        writes=([("ps", bu)] if last else []) +
                               ([("s6w1_f", sl)] if rel else []),
                        inc=ct if last else None)
                kb.op("scalar", lambda en, m2=m, b2=bu: en.activation(
                    s6_mid[:, m2 * TT6:(m2 + 1) * TT6], ps[b2][:], AF.Relu,
                    bias=s6_b1[:, m2:m2 + 1]),
                    reads=[("ps", bu), ("s6b1",), ("mid_f", m)],
                    writes=[("mid", m), ("psf", bu)], inc=cs)
        mid_keys = [("mid", m) for m in range(FC)]
        for tc in range(NTC6):
            for nhf in range(2):
                bd = 2 + (tc * 2 + nhf) % 2
                for k in range(FC):
                    last = (k == FC - 1)
                    kb.op("tensor", lambda en, t2=tc, n2=nhf, k2=k, b2=bd: en.matmul(
                        ps[b2][:],
                        s6_mid[:, k2 * TT6 + t2 * 128:k2 * TT6 + (t2 + 1) * 128],
                        s6_w2[:, k2 * D + n2 * 512:k2 * D + (n2 + 1) * 512],
                        start=(k2 == 0), stop=last),
                        reads=[("psf", bd), ("s6w2",)] + mid_keys,
                        writes=[("ps", bd)] if last else [],
                        inc=ct if last else None)
                kb.op("vector", lambda en, t2=tc, n2=nhf, b2=bd: en.tensor_copy(
                    s6_ys[t2 % 2][:, n2 * 512:(n2 + 1) * 512], ps[b2][:]),
                    reads=[("ps", bd), ("s6ys_f", tc % 2)],
                    writes=[("s6ysh", tc % 2, nhf), ("psf", bd)], inc=cv)
            kb.op("scalar", lambda en, j2=j, t2=tc: en.dma_start(
                out=ys_d[j2 * 512 + t2 * 128:j2 * 512 + (t2 + 1) * 128, :],
                in_=s6_ys[t2 % 2][:]),
                reads=[("s6ysh", tc % 2, 0), ("s6ysh", tc % 2, 1)],
                writes=[("ys", j, tc), ("s6ys_f", tc % 2)], inc=kb.dsem(f"ys{tc % 2}"), n=16)
        for m in range(FC):
            kb.mark(("mid_f", m), ct)
        kb.mark(("s6w2_f",), ct)
        kb.mark(("s6b1_f",), cs)
        kb.mark(("s6hn2T_f", st6), ct)

    ys_keys = [("ys", j, tc) for j in range(NTS) for tc in range(NTC6)]

    for i in range(NT6):
        t0 = i * TT6
        kb.op("scalar", lambda e, i2=i: e.dma_start(
            out=s6_mT[:], in_=maskT_d[:, i2 * TT6:(i2 + 1) * TT6]),
            reads=[("s6mT_f",)] + [("maskT", t0 // 128 + jj) for jj in range(NTC6)],
            writes=[("s6mT",)], inc=kb.dsem("s6mTl"), n=16)
        kb.op("scalar", lambda e, i2=i: e.dma_start(
            out=s6_s[:].rearrange("p (c t) -> p c t", c=DC),
            in_=s_d[:, i2 * TT6:(i2 + 1) * TT6].rearrange("(c p) t -> p c t", p=128)),
            reads=[("s6s_f",)] + [("s_d", m, t0 // T) for m in range(DC)],
            writes=[("s6s",)], inc=kb.dsem("s6sl"), n=16)
        for tc in range(NTC6):
            chunk = t0 // 128 + tc
            slc = tc % 2
            kb.op("scalar", lambda e, c2=chunk, s9=slc: e.dma_start(
                out=s6_gi[s9][:],
                in_=rank_d.rearrange("(c p) -> c p", p=128)[c2][:, None]),
                reads=[("s6gi_f", slc), ("rankd",)], writes=[("s6gi", slc)], inc=kb.dsem(f"s6gi{slc}"), n=16)
            kb.op("gpsimd", lambda e, s9=slc: e.indirect_dma_start(
                out=s6_yg[s9][:], out_offset=None,
                in_=ys_d[0:128, :], in_offset=IndirectOffsetOnAxis(s6_gi[s9][:], 0),
                bounds_check=breg(e, SLOTS - 1), oob_is_err=False),
                reads=[("s6gi", slc), ("s6yg_f", slc)] + ys_keys,
                writes=[("s6yg", slc)], inc=kb.dsem(f"s6yg{slc}"), n=16)
            if chk_yg is not None and i == 0 and tc == 0:
                kb.op("gpsimd", lambda e: e.dma_start(out=chk_yg, in_=s6_yg[0][:]),
                      reads=[("s6yg", 0)], writes=[("chkyg",)], inc=kb.dsem("chk"), n=16)
                kb.op("gpsimd", lambda e: e.dma_start(out=chk_gi, in_=s6_gi[0][:]),
                      reads=[("s6gi", 0)], writes=[("chkgi",)], inc=kb.dsem("chk"), n=16)
                kb.op("gpsimd", lambda e: e.dma_start(
                    out=chk_rank.rearrange("(a t) -> a t", a=1),
                    in_=rank_d.rearrange("(a t) -> a t", a=1)),
                    reads=[("rankd",)], writes=[("chkrk",)], inc=kb.dsem("chk"), n=16)
            kb.op("scalar", lambda e, c2=chunk, s9=slc: e.dma_start(
                out=s6_xp[s9][:], in_=xp_d[c2 * 128:(c2 + 1) * 128, :]),
                reads=[("s6xp_f", slc), ("xp", chunk // 4)], writes=[("s6xpl", slc)],
                inc=kb.dsem(f"s6xpl{slc}"), n=16)
            kb.op("scalar", lambda e, c2=chunk, s9=slc: e.dma_start(
                out=s6_x[s9][:], in_=x_in[c2 * 128:(c2 + 1) * 128, :]),
                reads=[("s6x_f", slc)], writes=[("s6xl", slc)], inc=kb.dsem(f"s6xl{slc}"), n=16)
            for nhf in range(2):
                bb = 4 + nhf
                kb.op("tensor", lambda e, t2=tc, n2=nhf, b2=bb: e.matmul(
                    ps[b2][:], s6_mT[:, t2 * 128:(t2 + 1) * 128],
                    b2e_sb[:, n2 * 512:(n2 + 1) * 512], start=True, stop=True),
                    reads=[("s6mT",), ("b2e",), ("psf", bb)],
                    writes=[("ps", bb)], inc=ct)
            for c in range(DC):
                tb = 6 + c // 4
                last4 = (c % 4 == 3)
                kb.op("tensor", lambda e, t2=tc, c2=c, b2=tb: e.matmul(
                    ps[b2][:, (c2 % 4) * 128:(c2 % 4 + 1) * 128],
                    s6_s[:, c2 * TT6 + t2 * 128:c2 * TT6 + (t2 + 1) * 128],
                    id_sb[:], is_transpose=True),
                    reads=[("s6s",), ("id",), ("psf", tb)],
                    writes=[("ps", tb)] if last4 else [],
                    inc=ct if last4 else None)
            for nhf in range(2):
                kb.op("vector", lambda e, n2=nhf, s9=slc: e.scalar_tensor_tensor(
                    s6_cmb[:, n2 * 512:(n2 + 1) * 512], ps[4 + n2][:], 1.0,
                    s6_yg[s9][:, n2 * 512:(n2 + 1) * 512], ALU.mult, ALU.add),
                    reads=[("ps", 4 + nhf), ("s6yg", slc), ("s6cmb_f",)],
                    writes=[("s6cmbh", nhf), ("psf", 4 + nhf), ("s6yg_f", slc)],
                    inc=cv)
            if chk_c is not None and i == 0 and tc == 0:
                kb.op("gpsimd", lambda e: e.dma_start(out=chk_c[0], in_=s6_cmb[:]),
                      reads=[("s6cmbh", 0), ("s6cmbh", 1)], writes=[("chk0",)], inc=kb.dsem("chk"), n=16)
            kb.op("vector", lambda e, s9=slc: e.tensor_add(
                s6_cmb[:], s6_cmb[:], s6_xp[s9][:]),
                reads=[("s6cmbh", 0), ("s6cmbh", 1), ("s6xpl", slc)]
                      + ([("chk0",)] if chk_c is not None and i == 0 and tc == 0 else []),
                writes=[("s6ffn",), ("s6xp_f", slc)], inc=cv)
            if chk_c is not None and i == 0 and tc == 0:
                kb.op("gpsimd", lambda e: e.dma_start(out=chk_c[1], in_=s6_cmb[:]),
                      reads=[("s6ffn",)], writes=[("chk1",)], inc=kb.dsem("chk"), n=16)
            kb.op("scalar", lambda e: e.activation(s6_g[:], s6_cmb[:], AF.Silu),
                  reads=[("s6ffn",), ("s6g_f",)]
                        + ([("chk1",)] if chk_c is not None and i == 0 and tc == 0 else []),
                  writes=[("s6g",), ("s6cmb_f",)], inc=cs)
            if chk_c is not None and i == 0 and tc == 0:
                kb.op("gpsimd", lambda e: e.dma_start(out=chk_c[2], in_=s6_g[:]),
                      reads=[("s6g",)], writes=[("chk2",)], inc=kb.dsem("chk"), n=16)
            for nhf in range(2):
                kb.op("vector", lambda e, n2=nhf: e.tensor_mul(
                    s6_g[:, n2 * 512:(n2 + 1) * 512], s6_g[:, n2 * 512:(n2 + 1) * 512],
                    ps[6 + n2][:]),
                    reads=[("s6g",), ("ps", 6 + nhf)],
                    writes=[("s6sg", nhf), ("psf", 6 + nhf)], inc=cv)
            if chk_c is not None and i == 0 and tc == 0:
                kb.op("gpsimd", lambda e: e.dma_start(out=chk_c[3], in_=s6_g[:]),
                      reads=[("s6sg", 0), ("s6sg", 1)], writes=[("chk3",)], inc=kb.dsem("chk"), n=16)
            kb.op("vector", lambda e, s9=slc: e.tensor_add(
                s6_out[:], s6_g[:], s6_x[s9][:]),
                reads=[("s6sg", 0), ("s6sg", 1), ("s6xl", slc), ("s6out_f",)]
                      + ([("chk3",)] if chk_c is not None and i == 0 and tc == 0 else []),
                writes=[("s6out",), ("s6g_f",), ("s6x_f", slc)], inc=cv)
            kb.op("gpsimd", lambda e, c2=chunk: e.dma_start(
                out=y_out[c2 * 128:(c2 + 1) * 128, :], in_=s6_out[:]),
                reads=[("s6out",)], writes=[("y", chunk), ("s6out_f",)], inc=kb.dsem("yout"), n=16)
            kb.mark(("s6gi_f", slc), kb.dsem(f"s6yg{slc}"))
        kb.mark(("s6mT_f",), ct)
        kb.mark(("s6s_f",), ct)

    kb.op("gpsimd", lambda e: e.engine_nop(),
          reads=[("y", ch) for ch in range(NT1)])
    kb.flush()
    sD.close()
    sD1.close()
    pers.close()
    ps_all_ctx.__exit__(None, None, None)
    for c in reversed(kb.sem_ctxs):
        c.__exit__(None, None, None)
    blk_ctx.__exit__(None, None, None)
    return nc


_NC_CACHE = {}


def _get_nc(SEQ, T):
    key = (SEQ, T)
    if key not in _NC_CACHE:
        _NC_CACHE[key] = build_program(SEQ, T)
    return _NC_CACHE[key]


def kernel(L0_attn_x, L0_attn_Wz, L0_attn_bz, L0_attn_Wh, L0_attn_bh,
           L0_ffn_ln1_g, L0_ffn_ln1_b, L0_ffn_ln2_g, L0_ffn_ln2_b,
           L0_ffn_wq, L0_ffn_bq, L0_ffn_wk, L0_ffn_bk, L0_ffn_wv, L0_ffn_bv,
           L0_ffn_wo, L0_ffn_bo, L0_ffn_gate_w, L0_ffn_gate_b,
           L0_ffn_w1, L0_ffn_b1, L0_ffn_w2, L0_ffn_b2):
    B, S, Dm = L0_attn_x.shape
    n_cores = 8
    SEQ = B // n_cores
    nc = _get_nc(SEQ, S)
    f = lambda a: np.ascontiguousarray(np.asarray(a, dtype=np.float32))
    shared = {
        "Wz": f(L0_attn_Wz), "bz": f(L0_attn_bz), "Wh": f(L0_attn_Wh), "bh": f(L0_attn_bh),
        "ln1_g": f(L0_ffn_ln1_g), "ln1_b": f(L0_ffn_ln1_b),
        "ln2_g": f(L0_ffn_ln2_g), "ln2_b": f(L0_ffn_ln2_b),
        "wq": f(L0_ffn_wq), "bq": f(L0_ffn_bq), "wk": f(L0_ffn_wk), "bk": f(L0_ffn_bk),
        "wv": f(L0_ffn_wv), "bv": f(L0_ffn_bv), "wo": f(L0_ffn_wo), "bo": f(L0_ffn_bo),
        "gate_w": f(L0_ffn_gate_w), "gate_b": f(L0_ffn_gate_b),
        "w1": f(L0_ffn_w1), "b1": f(L0_ffn_b1), "w2": f(L0_ffn_w2), "b2": f(L0_ffn_b2),
        "ident": np.eye(128, dtype=np.float32),
        "iotac": np.arange(128, dtype=np.float32)[:, None].copy(),
    }
    x = f(L0_attn_x)
    in_maps = []
    for c in range(n_cores):
        m = dict(shared)
        m["x"] = np.ascontiguousarray(x[c * SEQ:(c + 1) * SEQ].reshape(SEQ * S, Dm))
        in_maps.append(m)
    res = run_bass_kernel_spmd(nc, in_maps, core_ids=list(range(n_cores)))
    out = np.concatenate([r["y"].reshape(SEQ, S, Dm) for r in res.results], axis=0)
    return out


# version 10 (final)
# speedup vs baseline: 1.1475x; 1.1475x over previous
import numpy as np
import ml_dtypes

import concourse.bass as bass
from concourse.bass import IndirectOffsetOnAxis
import concourse.mybir as mybir
from concourse.bass_utils import run_bass_kernel_spmd

F32 = mybir.dt.float32
BF16 = mybir.dt.bfloat16
AF = mybir.ActivationFunctionType
ALU = mybir.AluOpType
AX = mybir.AxisListType
I32 = mybir.dt.int32

ENGINES = ("sync", "tensor", "vector", "scalar", "gpsimd")

D = 1024
DC = 8
NH = 16
DH = 64
E = 8
F = 4096
FC = 32
EPS = 1e-5


class KB:

    def __init__(self, nc, blk):
        self.nc = nc
        self.blk = blk
        self.ops = {e: [] for e in ENGINES}
        self.semh = {}
        self.semc = {}
        self.last_w = {}
        self.waited = {e: {} for e in ENGINES}
        self.sem_ctxs = []
        self.retire_gen = 0
        self.retire_keys = []
        self.retire_seen = {e: 0 for e in ENGINES}

    def sem(self, name):
        ctx = self.nc.semaphore(name)
        self.semh[name] = ctx.__enter__()
        self.sem_ctxs.append(ctx)
        self.semc[name] = 0
        return name

    def dsem(self, slot):
        name = "d_" + slot
        if name not in self.semh:
            self.sem(name)
        return name

    def op(self, eng, make, reads=(), writes=(), inc=None, n=1):
        reads = list(reads)
        if self.retire_seen[eng] < self.retire_gen:
            reads += self.retire_keys
            self.retire_seen[eng] = self.retire_gen
        wmax = {}
        for k in reads:
            lw = self.last_w.get(k)
            if lw is not None:
                s, c = lw
                if c > 0 and self.waited[eng].get(s, 0) < c and wmax.get(s, 0) < c:
                    wmax[s] = c
        waits = []
        for s, c in wmax.items():
            waits.append((s, c))
            self.waited[eng][s] = c
        stamp = None
        if inc is not None:
            self.semc[inc] += n
            stamp = self.semc[inc]
            for k in writes:
                self.last_w[k] = (inc, stamp)
        else:
            assert not writes, "tracked writes require an inc semaphore"
        self.ops[eng].append((tuple(waits), make, inc, n))
        return stamp

    def mark(self, key, sem):
        self.last_w[key] = (sem, self.semc[sem])

    def retire(self):
        self.retire_gen += 1
        g = self.retire_gen
        self.retire_keys = []
        for s in self.semc:
            k = ("ret", g, s)
            self.last_w[k] = (s, self.semc[s])
            self.retire_keys.append(k)

    def flush(self):
        for e in ENGINES:
            ops = self.ops[e]
            if not ops:
                continue
            self.ops[e] = []

            def body(engine, ops=ops):
                h = self.semh
                for waits, make, inc, n in ops:
                    for s, c in waits:
                        engine.wait_ge(h[s], c)
                    ins = make(engine)
                    if inc is not None:
                        ins.then_inc(h[inc], n)

            getattr(self.blk, e)(body)


class Scope:
    def __init__(self, nc):
        self.nc = nc
        self.ctxs = []

    def sb(self, name, shape, dt):
        ctx = self.nc.sbuf_tensor(name, shape, dt)
        t = ctx.__enter__()
        self.ctxs.append(ctx)
        return t

    def close(self):
        for c in reversed(self.ctxs):
            c.__exit__(None, None, None)
        self.ctxs = []


def build_program(SEQ, T, debug=False, debug2=False, expose=(), race_detect=True):
    TOK = SEQ * T
    assert T % 512 == 0
    NT1 = TOK // 128
    NT2 = TOK // 512
    TT6 = 512
    NT6 = TOK // TT6
    NTC6 = TT6 // 128
    KC = T // 128
    NW = T // 512

    nc = bass.Bass("TRN2", target_bir_lowering=False,
                   detect_race_conditions=race_detect)

    def din(name, shape, dt=F32):
        return nc.dram_tensor(name, shape, dt, kind="ExternalInput").ap()

    x_in = din("x", [TOK, D])
    Wz_in = din("Wz", [D, D]); bz_in = din("bz", [D])
    Wh_in = din("Wh", [D, D]); bh_in = din("bh", [D])
    g1_in = din("ln1_g", [D]); be1_in = din("ln1_b", [D])
    g2_in = din("ln2_g", [D]); be2_in = din("ln2_b", [D])
    wq_in = din("wq", [D, D]); bq_in = din("bq", [D])
    wk_in = din("wk", [D, D]); bk_in = din("bk", [D])
    wv_in = din("wv", [D, D]); bv_in = din("bv", [D])
    wo_in = din("wo", [D, D]); bo_in = din("bo", [D])
    gw_in = din("gate_w", [D, E]); gb_in = din("gate_b", [E])
    w1_in = din("w1", [E, D, F]); b1_in = din("b1", [E, F])
    w2_in = din("w2", [E, F, D]); b2_in = din("b2", [E, D])
    id_in = din("ident", [128, 128])
    iota_in = din("iotac", [128, 1])

    y_out = nc.dram_tensor("y", [TOK, D], F32, kind="ExternalOutput").ap()

    def scr(name, shape, dt):
        kind = "ExternalOutput" if debug else "Internal"
        return nc.dram_tensor(name, shape, dt, kind=kind).ap()

    xnbo_d = scr("xnbo", [TOK, D], F32)
    xnb_d = scr("xnb", [TOK, D], BF16)
    hnb_d = scr("hnb", [TOK, D], BF16)
    a_d = scr("a_fm", [D, TOK], F32)
    b_d = scr("b_fm", [D, TOK], F32)
    s_d = scr("s_fm", [D, TOK], F32)
    qT_d = scr("qT", [D, TOK], BF16)
    kT_d = scr("kT", [D, TOK], BF16)
    v_d = scr("v_tok", [TOK, D], BF16)
    oT_d = scr("oT", [D, TOK], BF16)
    xp_d = scr("xp", [TOK, D], F32)
    hn2_d = scr("hn2b", [TOK, D], BF16)
    mask_d = scr("mask_c", [TOK // 128, 128, E], F32)
    maskT_d = scr("maskT_c", [E, TOK], F32)
    NTS = TOK // 512 + E - 1
    SLOTS = NTS * 512
    w1s_d = scr("w1s", [E, 128, FC, DC * 128], BF16)
    w2s_d = scr("w2s", [E, 128, FC * D], BF16)
    b1s_d = scr("b1s", [E, 128, FC], F32)
    def scr2(name, shape, dt):
        kind = "ExternalOutput" if (debug or debug2 or name in expose) else "Internal"
        return nc.dram_tensor(name, shape, dt, kind=kind).ap()
    if debug2 or debug:
        pass
    chk_rank = nc.dram_tensor("chk_rank", [TOK], I32, kind="ExternalOutput").ap() \
        if debug or debug2 or ("chk" in expose) else None
    chk_yg = nc.dram_tensor("chk_yg", [128, D], F32, kind="ExternalOutput").ap() \
        if debug or debug2 or ("chk" in expose) else None
    chk_gi = nc.dram_tensor("chk_gi", [128, 1], I32, kind="ExternalOutput").ap() \
        if debug or debug2 or ("chk" in expose) else None
    chk_c = nc.dram_tensor("chk_c", [4, 128, D], F32, kind="ExternalOutput").ap() \
        if debug or debug2 or ("chk" in expose) else None
    hn2s_d = scr2("hn2s", [SLOTS, D], BF16)
    ys_d = scr2("ys", [SLOTS, D], F32)
    rank_d = scr2("rank_d", [TOK], I32)
    if debug:
        dbg_acc = scr("dbg_acc", [TOK, D], F32)
        dbg_mid = scr("dbg_mid", [E, F, TT6], BF16)
        dbg_acce = scr("dbg_acce", [E, TT6, D], F32)

    blk_ctx = nc.Block()
    blk = blk_ctx.__enter__()
    kb = KB(nc, blk)
    qs = kb.sem("qs")
    qg = kb.sem("qg")
    qc = kb.sem("qc")
    ct = kb.sem("ct")
    cv = kb.sem("cv")
    cs = kb.sem("cs")
    cg = kb.sem("cg")

    pers = Scope(nc)
    id_sb = pers.sb("id_sb", [128, 128], F32)
    ones_row = pers.sb("ones_row", [1, 128], F32)
    ones8 = pers.sb("ones8", [128, 8], F32)
    bz_sb = pers.sb("bz_sb", [128, DC], F32)
    bh_sb = pers.sb("bh_sb", [128, DC], F32)
    bq_sb = pers.sb("bq_sb", [128, DC], F32)
    bk_sb = pers.sb("bk_sb", [128, DC], F32)
    ln_tmp = {
        "s1": pers.sb("ln_s1", [128, 4], F32),
        "s2": pers.sb("ln_s2", [128, 4], F32),
        "sq": pers.sb("ln_sq", [128, D], F32),
        "m": pers.sb("ln_m", [128, 4], F32),
        "sd": pers.sb("ln_sd", [128, 4], F32),
        "r": pers.sb("ln_r", [128, 4], F32),
        "nmr": pers.sb("ln_nmr", [128, 4], F32),
    }
    bias_stg = pers.sb("bias_stg", [128, 128], F32)
    ps_all_ctx = nc.psum_tensor("ps_all", [128, 4096], F32)
    ps_all = ps_all_ctx.__enter__()
    ps_ctx = []
    ps = [ps_all[:, i * 512:(i + 1) * 512] for i in range(8)]

    def load_colmajor(dst_ap, src_2d, n):
        assert n <= 128
        kb.op("sync", lambda e: e.dma_start(out=bias_stg[0:n, :], in_=src_2d),
              reads=[("bstg_f",)], writes=[("bstg",)], inc=kb.dsem("bstg"), n=16)
        kb.op("tensor", lambda e: e.matmul(
            ps[7][:, 0:n], bias_stg[0:n, :], id_sb[0:n, 0:n], is_transpose=True),
            reads=[("bstg",), ("id",), ("psf", 7)], writes=[("ps", 7)], inc=ct)
        kb.op("vector", lambda e: e.tensor_copy(dst_ap, ps[7][:, 0:n]),
              reads=[("ps", 7)], writes=[("psf", 7), ("bstg_f",)], inc=cv)

    def bcast_row(scope, nm, src, width=D):
        tile = scope.sb(nm, [128, width], F32)
        row = bc_row_tile[0]
        nhalf = (width + 511) // 512
        kb.op("sync", lambda e, s2=src, w2=width: e.dma_start(
            out=row[:, 0:w2], in_=s2[None, :]),
            reads=[("bcrow_f",)], writes=[("bcrow",)], inc=kb.dsem("bcrow"), n=16)
        for h in range(nhalf):
            w0 = h * 512
            w1 = min(width, w0 + 512)
            kb.op("tensor", lambda e, a=w0, b=w1: e.matmul(
                ps[0][:, 0:b - a], ones_row[:], row[:, a:b], start=True, stop=True),
                reads=[("bcrow",), ("ones_row",), ("psf", 0)],
                writes=[("ps", 0)], inc=ct)
            kb.op("vector", lambda e, a=w0, b=w1, t2=tile: e.tensor_copy(
                t2[:, a:b], ps[0][:, 0:b - a]),
                reads=[("ps", 0)],
                writes=[(nm, h), ("psf", 0), ("bcrow_f",)], inc=cv)
        return tile, [(nm, h) for h in range(nhalf)]

    def emit_ln4(src_flat, src_keys, out_flat, out_key, extra_apply_reads=()):
        s1t, s2t, sq = ln_tmp["s1"], ln_tmp["s2"], ln_tmp["sq"]
        mt, sdt, rt, nmrt = ln_tmp["m"], ln_tmp["sd"], ln_tmp["r"], ln_tmp["nmr"]
        src4 = src_flat.rearrange("p (t d) -> p t d", t=4)
        kb.op("vector", lambda e: e.reduce_sum(s1t[:], src4, axis=AX.X),
              reads=list(src_keys), writes=[("lns1",)], inc=cv)
        for c in range(4):
            kb.op("scalar", lambda e, c2=c: e.activation(
                sq[:], src_flat[:, c2 * D:(c2 + 1) * D], AF.Square,
                accum_out=s2t[:, c2:c2 + 1]),
                reads=list(src_keys) + [("lnsq_f",), ("lnsq",)],
                writes=[("lnsq",)] if c < 3 else [("lnsq",), ("lns2",)], inc=cs)
        kb.mark(("lnsq_f",), cs)
        kb.op("vector", lambda e: e.tensor_single_scalar(mt[:], s1t[:], 1.0 / D, ALU.mult),
              reads=[("lns1",)], writes=[("lnm",)], inc=cv)
        kb.op("vector", lambda e: e.tensor_single_scalar(s2t[:], s2t[:], 1.0 / D, ALU.mult),
              reads=[("lns2",)], writes=[("lnv",)], inc=cv)
        kb.op("vector", lambda e: e.tensor_tensor(
            sdt[:], mt[:], mt[:], ALU.mult),
            reads=[("lnm",)], writes=[("lnmsq",)], inc=cv)
        kb.op("vector", lambda e: e.tensor_tensor(
            sdt[:], s2t[:], sdt[:], ALU.subtract),
            reads=[("lnmsq",), ("lnv",)], writes=[("lnvar",)], inc=cv)
        kb.op("vector", lambda e: e.tensor_single_scalar(sdt[:], sdt[:], EPS, ALU.add),
              reads=[("lnvar",)], writes=[("lnvpe",)], inc=cv)
        kb.op("scalar", lambda e: e.activation(sdt[:], sdt[:], AF.Sqrt),
              reads=[("lnvpe",)], writes=[("lnsd",)], inc=cs)
        kb.op("vector", lambda e: e.reciprocal(rt[:], sdt[:]),
              reads=[("lnsd",)], writes=[("lnr",)], inc=cv)
        kb.op("vector", lambda e: e.scalar_tensor_tensor(
            nmrt[:], mt[:], -1.0, rt[:], ALU.mult, ALU.mult),
            reads=[("lnr",), ("lnm",)], writes=[("lnnmr",)], inc=cv)
        for c in range(4):
            kb.op("scalar", lambda e, c2=c: e.activation(
                out_flat[:, c2 * D:(c2 + 1) * D], src_flat[:, c2 * D:(c2 + 1) * D],
                AF.Identity, bias=nmrt[:, c2:c2 + 1], scale=rt[:, c2:c2 + 1]),
                reads=[("lnnmr",), ("lnr",)] + list(src_keys) + list(extra_apply_reads),
                writes=[out_key, (out_key[0] + "_c", c)], inc=cs)

    kb.op("sync", lambda e: e.dma_start(out=id_sb[:], in_=id_in),
          writes=[("id",)], inc=kb.dsem("idc"), n=16)
    kb.op("vector", lambda e: e.memset(ones_row[:], 1.0), writes=[("ones_row",)], inc=cv)
    kb.op("vector", lambda e: e.memset(ones8[:], 1.0), writes=[("ones8",)], inc=cv)
    for dst, src, key in ((bz_sb, bz_in, "bz"), (bh_sb, bh_in, "bh"),
                          (bq_sb, bq_in, "bq"), (bk_sb, bk_in, "bk")):
        load_colmajor(dst[:], src.rearrange("(c p) -> c p", p=128), DC)
        kb.mark((key,), cv)

    sA = Scope(nc)
    bc_row_tile = [sA.sb("bcrowA", [1, D], F32)]
    g1b, g1b_k = bcast_row(sA, "g1b", g1_in)
    be1b, be1b_k = bcast_row(sA, "be1b", be1_in)
    bvb, bvb_k = bcast_row(sA, "bvb", bv_in)
    bob, bob_k = bcast_row(sA, "bob", bo_in)

    s2w = {}
    for nm, src in (("Wz", Wz_in), ("Wh", Wh_in), ("wq", wq_in),
                    ("wk", wk_in), ("wv", wv_in)):
        s2w[nm] = sA.sb(f"s2w_{nm}", [128, DC * D], BF16)
        kb.op("gpsimd", lambda e, n2=nm, s2=src: e.dma_start(
            out=s2w[n2][:].rearrange("p (k n) -> p k n", k=DC),
            in_=s2.rearrange("(k p) n -> p k n", p=128)),
            writes=[("s2w", nm)], inc=kb.dsem("s2w_" + nm), n=16)

    s1_x = [sA.sb("s1x0", [128, 4 * D], F32)]
    s1_xn = sA.sb("s1xn", [128, 4 * D], F32)
    s1_hn = sA.sb("s1hn", [128, 4 * D], F32)
    s1_xnbo = s1_x[0]

    def emit_s1_macro(i):
        sl = 0
        kb.op("sync", lambda e, i2=i, s2=sl: e.dma_start(
            out=s1_x[s2][:].rearrange("p (t d) -> p t d", t=4),
            in_=x_in[i2 * 512:(i2 + 1) * 512, :].rearrange("(t p) d -> p t d", p=128)),
            reads=[("s1xf", sl)], writes=[("s1x", sl)], inc=kb.dsem("s1x"), n=16)
        extra = [("xnb", i - 1), ("s1xn_ov",)] if i > 0 else []
        emit_ln4(s1_x[sl][:], [("s1x", sl)], s1_xn[:], ("s1xn",),
                 extra_apply_reads=extra)
        for c in range(4):
            kb.op("vector", lambda e, c2=c: e.tensor_add(
                s1_xnbo[:, c2 * D:(c2 + 1) * D], s1_xn[:, c2 * D:(c2 + 1) * D], bob[:]),
                reads=[("s1xn_c", c), ("s1ln2_c", c), ("s1xnbo_f",)] + bob_k,
                writes=[("s1xnbo",), ("s1xnbo_c", c)], inc=cv)
        kb.op("scalar", lambda e, i2=i: e.dma_start(
            out=xnbo_d[i2 * 512:(i2 + 1) * 512, :].rearrange("(t p) d -> p t d", p=128),
            in_=s1_xnbo[:].rearrange("p (t d) -> p t d", t=4)),
            reads=[("s1xnbo",)], writes=[("xnbo", i), ("s1xnbo_f",), ("s1xf", sl)],
            inc=kb.dsem("s1xnbo"), n=16)
        kb.op("gpsimd", lambda e, i2=i: e.dma_start(
            out=xnb_d[i2 * 512:(i2 + 1) * 512, :].rearrange("(t p) d -> p t d", p=128),
            in_=s1_xn[:].rearrange("p (t d) -> p t d", t=4)),
            reads=[("s1xn",)], writes=[("xnb", i)], inc=kb.dsem("s1xnst"), n=16)
        extra_h = [("hnb", i - 1)] if i > 0 else []
        emit_ln4(s1_xn[:], [("s1xn",)], s1_hn[:], ("s1ln2",),
                 extra_apply_reads=extra_h)
        kb.mark(("s1xn_ov",), cs)
        for c in range(4):
            kb.op("vector", lambda e, c2=c: e.tensor_mul(
                s1_hn[:, c2 * D:(c2 + 1) * D], s1_hn[:, c2 * D:(c2 + 1) * D], g1b[:]),
                reads=[("s1ln2_c", c)] + g1b_k, writes=[("s1hg", c)], inc=cv)
            kb.op("vector", lambda e, c2=c: e.tensor_add(
                s1_hn[:, c2 * D:(c2 + 1) * D], s1_hn[:, c2 * D:(c2 + 1) * D], be1b[:]),
                reads=[("s1hg", c)] + be1b_k, writes=[("s1hn",), ("s1hn_c", c)], inc=cv)
        kb.op("gpsimd", lambda e, i2=i: e.dma_start(
            out=hnb_d[i2 * 512:(i2 + 1) * 512, :].rearrange("(t p) d -> p t d", p=128),
            in_=s1_hn[:].rearrange("p (t d) -> p t d", t=4)),
            reads=[("s1hn",)] + [("s1hn_c", c) for c in range(4)],
            writes=[("hnb", i)], inc=kb.dsem("s1hnst"), n=16)

    s2_xnT = [sA.sb(f"s2xnT{i}", [128, DC * 512], BF16) for i in range(2)]
    s2_hnT = [sA.sb(f"s2hnT{i}", [128, DC * 512], BF16) for i in range(2)]
    s2_z = sA.sb("s2z", [128, 512], F32)
    s2_a = sA.sb("s2a", [128, 512], F32)
    s2_ht = sA.sb("s2ht", [128, 512], F32)
    s2_b = sA.sb("s2b", [128, 512], F32)
    s2_qe = sA.sb("s2qe", [128, 512], BF16)
    s2_ke = sA.sb("s2ke", [128, 512], BF16)
    s2_ve = sA.sb("s2ve", [128, 512], BF16)
    bank_ctr = [0]

    def nb():
        b = bank_ctr[0] % 6
        bank_ctr[0] += 1
        return b

    def emit_s2_tile(i):
        sl = i % 2
        t0 = i * 512
        t128 = [t0 // 128 + j for j in range(4)]
        for c in range(DC):
            kb.op("sync", lambda e, i2=i, c2=c, s2=sl: e.dma_start(
                out=s2_xnT[s2][:, c2 * 512:(c2 + 1) * 512],
                in_=xnb_d[i2 * 512:(i2 + 1) * 512, c2 * 128:(c2 + 1) * 128],
                transpose=True),
                reads=[("s2xnT_f", sl), ("xnb", i)],
                writes=[("s2xnT", sl, c)], inc=kb.dsem(f"s2xnT{sl}"), n=16)
            kb.op("sync", lambda e, i2=i, c2=c, s2=sl: e.dma_start(
                out=s2_hnT[s2][:, c2 * 512:(c2 + 1) * 512],
                in_=hnb_d[i2 * 512:(i2 + 1) * 512, c2 * 128:(c2 + 1) * 128],
                transpose=True),
                reads=[("s2hnT_f", sl), ("hnb", i)],
                writes=[("s2hnT", sl, c)], inc=kb.dsem(f"s2hnT{sl}"), n=16)
        xnT_keys = [("s2xnT", sl, c) for c in range(DC)]
        hnT_keys = [("s2hnT", sl, c) for c in range(DC)]

        for m in range(DC):
            bz_ = nb(); bh_ = nb()
            for k in range(DC):
                kb.op("tensor", lambda e, m2=m, k2=k, b2=bz_, s2=sl: e.matmul(
                    ps[b2][:], s2w["Wz"][:, k2 * D + m2 * 128:k2 * D + (m2 + 1) * 128],
                    s2_xnT[s2][:, k2 * 512:(k2 + 1) * 512],
                    start=(k2 == 0), stop=(k2 == DC - 1)),
                    reads=[("s2w", "Wz"), ("psf", bz_)] + xnT_keys,
                    writes=[("ps", bz_)] if k == DC - 1 else [],
                    inc=ct if k == DC - 1 else None)
            for k in range(DC):
                kb.op("tensor", lambda e, m2=m, k2=k, b2=bh_, s2=sl: e.matmul(
                    ps[b2][:], s2w["Wh"][:, k2 * D + m2 * 128:k2 * D + (m2 + 1) * 128],
                    s2_xnT[s2][:, k2 * 512:(k2 + 1) * 512],
                    start=(k2 == 0), stop=(k2 == DC - 1)),
                    reads=[("s2w", "Wh"), ("psf", bh_)] + xnT_keys,
                    writes=[("ps", bh_)] if k == DC - 1 else [],
                    inc=ct if k == DC - 1 else None)
            kb.op("scalar", lambda e, m2=m, b2=bz_: e.activation(
                s2_z[:], ps[b2][:], AF.Sigmoid, bias=bz_sb[:, m2:m2 + 1]),
                reads=[("ps", bz_), ("bz",), ("s2zf",)], writes=[("s2z",)], inc=cs)
            kb.op("scalar", lambda e, m2=m, b2=bh_: e.activation(
                s2_ht[:], ps[b2][:], AF.Identity, bias=bh_sb[:, m2:m2 + 1]),
                reads=[("ps", bh_), ("bh",), ("s2htf",)], writes=[("s2ht",)], inc=cs)
            kb.op("vector", lambda e: e.tensor_scalar(
                s2_a[:], s2_z[:], -1.0, 1.0, ALU.mult, ALU.add),
                reads=[("s2z",), ("s2af",)], writes=[("s2a",), ("psf", bz_)], inc=cv)
            kb.op("vector", lambda e: e.tensor_mul(s2_b[:], s2_z[:], s2_ht[:]),
                reads=[("s2z",), ("s2ht",), ("s2bf",)],
                writes=[("s2b",), ("s2zf",), ("s2htf",), ("psf", bh_)], inc=cv)
            kb.op("scalar", lambda e, m2=m, i2=i: e.dma_start(
                out=a_d[m2 * 128:(m2 + 1) * 128, i2 * 512:(i2 + 1) * 512], in_=s2_a[:]),
                reads=[("s2a",)], writes=[("a_d", m, i), ("s2af",)], inc=kb.dsem("s2ast"), n=16)
            kb.op("scalar", lambda e, m2=m, i2=i: e.dma_start(
                out=b_d[m2 * 128:(m2 + 1) * 128, i2 * 512:(i2 + 1) * 512], in_=s2_b[:]),
                reads=[("s2b",)], writes=[("b_d", m, i), ("s2bf",)], inc=kb.dsem("s2bst"), n=16)

        for wname, bsb, bkey, esb, ekey, dscr, dkey in (
                ("wq", bq_sb, "bq", s2_qe, "s2qe", qT_d, "qT"),
                ("wk", bk_sb, "bk", s2_ke, "s2ke", kT_d, "kT")):
            for m in range(DC):
                bq_ = nb()
                for k in range(DC):
                    kb.op("tensor", lambda e, w2=wname, m2=m, k2=k, b2=bq_, s2=sl: e.matmul(
                        ps[b2][:], s2w[w2][:, k2 * D + m2 * 128:k2 * D + (m2 + 1) * 128],
                        s2_hnT[s2][:, k2 * 512:(k2 + 1) * 512],
                        start=(k2 == 0), stop=(k2 == DC - 1)),
                        reads=[("s2w", wname), ("psf", bq_)] + hnT_keys,
                        writes=[("ps", bq_)] if k == DC - 1 else [],
                        inc=ct if k == DC - 1 else None)
                kb.op("scalar", lambda e, m2=m, b2=bq_, eb=esb, bb=bsb: e.activation(
                    eb[:], ps[b2][:], AF.Identity, bias=bb[:, m2:m2 + 1]),
                    reads=[("ps", bq_), (bkey,), (ekey + "f",)], writes=[(ekey,)], inc=cs)
                kb.op("sync", lambda e, m2=m, i2=i, eb=esb, dd=dscr: e.dma_start(
                    out=dd[m2 * 128:(m2 + 1) * 128, i2 * 512:(i2 + 1) * 512], in_=eb[:]),
                    reads=[(ekey,)], writes=[(dkey, m, i), (ekey + "f",), ("psf", bq_)],
                    inc=kb.dsem(ekey + "st"), n=16)

        for tc in range(4):
            for nhf in range(2):
                bv_ = nb()
                for k in range(DC):
                    kb.op("tensor", lambda e, t2=tc, n2=nhf, k2=k, b2=bv_, s2=sl: e.matmul(
                        ps[b2][:],
                        s2_hnT[s2][:, k2 * 512 + t2 * 128:k2 * 512 + (t2 + 1) * 128],
                        s2w["wv"][:, k2 * D + n2 * 512:k2 * D + (n2 + 1) * 512],
                        start=(k2 == 0), stop=(k2 == DC - 1)),
                        reads=[("s2w", "wv"), ("psf", bv_)] + hnT_keys,
                        writes=[("ps", bv_)] if k == DC - 1 else [],
                        inc=ct if k == DC - 1 else None)
                kb.op("vector", lambda e, n2=nhf, b2=bv_: e.tensor_add(
                    s2_ve[:], ps[b2][:], bvb[:, n2 * 512:(n2 + 1) * 512]),
                    reads=[("ps", bv_), ("s2vef",)] + bvb_k,
                    writes=[("s2ve",)], inc=cv)
                kb.op("gpsimd", lambda e, t2=tc, n2=nhf, i2=i: e.dma_start(
                    out=v_d[i2 * 512 + t2 * 128:i2 * 512 + (t2 + 1) * 128,
                            n2 * 512:(n2 + 1) * 512],
                    in_=s2_ve[:]),
                    reads=[("s2ve",)],
                    writes=[("v_d", i, tc, nhf), ("s2vef",), ("psf", bv_)], inc=kb.dsem("s2vest"), n=16)

        kb.mark(("s2xnT_f", sl), ct)
        kb.mark(("s2hnT_f", sl), ct)

    emit_s1_macro(0)
    for _i2 in range(NT2):
        if _i2 + 1 < NT2:
            emit_s1_macro(_i2 + 1)
        emit_s2_tile(_i2)

    kb.flush()
    sA.close()
    kb.retire()

    sB = Scope(nc)
    s3_a = [sB.sb("s3a0", [128, T], F32)]
    s3_b = [sB.sb("s3b0", [128, T], F32)]
    s3_s = [sB.sb("s3s0", [128, T], F32)]
    s2ps = T // 512
    it = 0
    for sq_ in range(SEQ):
        for m in range(DC):
            sl = 0
            it += 1
            i_range = [sq_ * s2ps + j for j in range(s2ps)]
            kb.op("scalar", lambda e, s2=sl, m2=m, q2=sq_: e.dma_start(
                out=s3_a[s2][:], in_=a_d[m2 * 128:(m2 + 1) * 128, q2 * T:(q2 + 1) * T]),
                reads=[("s3af", sl)] + [("a_d", m, i3) for i3 in i_range],
                writes=[("s3a", sl)], inc=kb.dsem("s3a"), n=16)
            kb.op("scalar", lambda e, s2=sl, m2=m, q2=sq_: e.dma_start(
                out=s3_b[s2][:], in_=b_d[m2 * 128:(m2 + 1) * 128, q2 * T:(q2 + 1) * T]),
                reads=[("s3bf", sl)] + [("b_d", m, i3) for i3 in i_range],
                writes=[("s3b", sl)], inc=kb.dsem("s3b"), n=16)
            kb.op("vector", lambda e, s2=sl: e.tensor_tensor_scan(
                s3_s[s2][:], s3_a[s2][:], s3_b[s2][:], 0.0, ALU.mult, ALU.add),
                reads=[("s3a", sl), ("s3b", sl), ("s3sf", sl)],
                writes=[("s3s", sl), ("s3af", sl), ("s3bf", sl)], inc=cv)
            kb.op("gpsimd", lambda e, s2=sl, m2=m, q2=sq_: e.dma_start(
                out=s_d[m2 * 128:(m2 + 1) * 128, q2 * T:(q2 + 1) * T], in_=s3_s[s2][:]),
                reads=[("s3s", sl)], writes=[("s_d", m, sq_), ("s3sf", sl)], inc=kb.dsem("s3sst"), n=16)


    s0_tmpb = [sB.sb(f"s0b{i}", [128, 2048], BF16) for i in range(2)]
    w1r = w1_in.rearrange("e (k p) (m c) -> e m p k c", p=128, c=128)
    s0_jobs = [("w1", e_, m) for e_ in range(E) for m in range(FC)] + \
              [("w2", e_, k) for e_ in range(E) for k in range(0, FC, 2)]
    for it0, (kind, e_, m) in enumerate(s0_jobs):
        sl = it0 % 2
        if kind == "w1":
            kb.op("gpsimd", lambda en, e2=e_, m2=m, s2=sl: en.dma_start(
                out=s0_tmpb[s2][:, 0:1024].rearrange("p (k c) -> p k c", k=DC),
                in_=w1r[e2, m2]),
                reads=[("s0cf", sl)], writes=[("s0c", sl)], inc=kb.dsem(f"s0l{sl}"), n=16)
            kb.op("gpsimd", lambda en, e2=e_, m2=m, s2=sl: en.dma_start(
                out=w1s_d[e2, :, m2, :], in_=s0_tmpb[s2][:, 0:1024]),
                reads=[("s0c", sl)], writes=[("w1s", e_, m), ("s0cf", sl)], inc=kb.dsem(f"s0s{sl}"), n=16)
        else:
            kb.op("gpsimd", lambda en, e2=e_, k2=m, s2=sl: en.dma_start(
                out=s0_tmpb[s2][:].rearrange("p (j n) -> p j n", j=2),
                in_=w2_in[e2, k2 * 128:(k2 + 2) * 128, :]
                    .rearrange("(j p) n -> p j n", p=128)),
                reads=[("s0cf", sl)], writes=[("s0c", sl)], inc=kb.dsem(f"s0l{sl}"), n=16)
            kb.op("gpsimd", lambda en, e2=e_, k2=m, s2=sl: en.dma_start(
                out=w2s_d[e2, :, k2 * 1024:(k2 + 2) * 1024],
                in_=s0_tmpb[s2][:]),
                reads=[("s0c", sl)],
                writes=[("w2s", e_, m), ("w2s", e_, m + 1), ("s0cf", sl)], inc=kb.dsem(f"s0s{sl}"), n=16)

    kb.mark(("w1s_done", 0), kb.dsem("s0s0"))
    kb.mark(("w1s_done", 1), kb.dsem("s0s1"))

    s4_qf = sB.sb("s4qf", [128, DC * T], BF16)
    s4_kf = sB.sb("s4kf", [128, DC * T], BF16)
    s4_vaug = sB.sb("s4vaug", [128, KC * NH * 65], BF16)
    s4_vstg = sB.sb("s4vstg", [128, 4 * D], BF16)
    s4_probs = sB.sb("s4probs", [128, 2 * KC * 512], BF16)
    s4_zrow = [sB.sb(f"s4zrow{i}", [1, 512], F32) for i in range(2)]
    s4_rz = [sB.sb(f"s4rz{i}", [1, 512], F32) for i in range(2)]
    s4_of = sB.sb("s4of", [64, 512], F32)
    s4_ob = sB.sb("s4ob", [64, 512], BF16)
    kb.op("vector", lambda e: e.memset(
        s4_vaug[:].rearrange("p (kh c) -> p kh c", c=65)[:, :, 64:65], 1.0),
        writes=[("vaug_ones",)], inc=cv)
    for sq_ in range(SEQ):
        i_range = [sq_ * s2ps + j for j in range(s2ps)]
        kb.op("sync", lambda e, q2=sq_: e.dma_start(
            out=s4_qf[:].rearrange("p (c t) -> p c t", c=DC),
            in_=qT_d[:, q2 * T:(q2 + 1) * T].rearrange("(c p) t -> p c t", p=128)),
            reads=[("s4qf_f",)] + [("qT", m, i3) for m in range(DC) for i3 in i_range],
            writes=[("s4qf",)], inc=kb.dsem("s4qf"), n=16)
        kb.op("sync", lambda e, q2=sq_: e.dma_start(
            out=s4_kf[:].rearrange("p (c t) -> p c t", c=DC),
            in_=kT_d[:, q2 * T:(q2 + 1) * T].rearrange("(c p) t -> p c t", p=128)),
            reads=[("s4kf_f",)] + [("kT", m, i3) for m in range(DC) for i3 in i_range],
            writes=[("s4kf",)], inc=kb.dsem("s4kf"), n=16)
        for vi in range(KC // 4):
            v_keys = [("v_d", i3, tc, nhf) for i3 in i_range for tc in range(4)
                      for nhf in range(2)]
            kb.op("sync", lambda e, q2=sq_, v2=vi: e.dma_start(
                out=s4_vstg[:].rearrange("p (c d) -> p c d", c=4),
                in_=v_d[q2 * T + v2 * 512:q2 * T + (v2 + 1) * 512, :]
                    .rearrange("(c p) d -> p c d", p=128)),
                reads=[("s4vstg_f",)] + v_keys, writes=[("s4vstg",)], inc=kb.dsem("s4vstg"), n=16)
            kb.op("vector", lambda e, v2=vi: e.tensor_copy(
                s4_vaug[:, v2 * 4 * NH * 65:(v2 + 1) * 4 * NH * 65]
                    .rearrange("p (k h c) -> p k h c", k=4, h=NH)[:, :, :, 0:64],
                s4_vstg[:].rearrange("p (k h d) -> p k h d", k=4, h=NH)),
                reads=[("s4vstg",), ("vaug_ones",), ("s4vaug_f", vi)],
                writes=[("s4vaug", vi), ("s4vstg_f",)], inc=cv)
        vaug_keys = [("s4vaug", vi) for vi in range(KC // 4)]

        def score_one(cur, kc):
            h, w, par = cur
            pb = 64 * (h % 2)
            chh = h // 2
            bsc = kc % 4
            kb.op("tensor", lambda e, kc2=kc, b2=bsc, pb2=pb, c2=chh, w2=w: e.matmul(
                ps[b2][:],
                s4_kf[pb2:pb2 + 64, c2 * T + kc2 * 128:c2 * T + (kc2 + 1) * 128],
                s4_qf[pb2:pb2 + 64, c2 * T + w2 * 512:c2 * T + (w2 + 1) * 512],
                start=True, stop=True),
                reads=[("s4qf",), ("s4kf",), ("psf", bsc)],
                writes=[("ps", bsc)], inc=ct)
            if kc % 2 == 1:
                b0 = (kc - 1) % 4
                kb.op("scalar", lambda e, kc2=kc, b2=b0, p2=par: e.activation(
                    s4_probs[:, (p2 * KC + kc2 - 1) * 512:(p2 * KC + kc2 + 1) * 512],
                    ps_all[:, b2 * 512:(b2 + 2) * 512], AF.Exp, scale=0.125),
                    reads=[("ps", b0), ("ps", b0 + 1),
                           ("probs_f", par, kc - 1), ("probs_f", par, kc)],
                    writes=[("probs", par, kc - 1), ("probs", par, kc),
                            ("psf", b0), ("psf", b0 + 1)], inc=cs)

        def attnv_one(p, kc):
            h, w, par = p
            ow = 4 + par
            last = (kc == KC - 1)
            kb.op("tensor", lambda e, h2=h, kc2=kc, o2=ow, p2=par: e.matmul(
                ps[o2][0:65, :],
                s4_vaug[:, kc2 * NH * 65 + h2 * 65:kc2 * NH * 65 + (h2 + 1) * 65],
                s4_probs[:, (p2 * KC + kc2) * 512:(p2 * KC + kc2 + 1) * 512],
                start=(kc2 == 0), stop=last),
                reads=[("probs", par, kc), ("psf", ow)] + vaug_keys,
                writes=[("ps", ow)] if last else [],
                inc=ct if last else None)

        def norm_a(p):
            h, w, par = p
            ow = 4 + par
            kb.op("vector", lambda e, o2=ow, p2=par: e.tensor_copy(
                s4_zrow[p2][:], ps[o2][64:65, :]),
                reads=[("ps", ow), ("zrow_f", par)], writes=[("zrow", par)], inc=cv)
            kb.op("vector", lambda e, p2=par: e.reciprocal(s4_rz[p2][:], s4_zrow[p2][:]),
                  reads=[("zrow", par), ("rz_f", par)],
                  writes=[("rz", par), ("zrow_f", par)], inc=cv)

        def norm_b(p, sq2):
            h, w, par = p
            ow = 4 + par
            orz = 6 + par
            kb.op("tensor", lambda e, o2=orz, p2=par: e.matmul(
                ps[o2][0:64, :], ones_row[:, 0:64], s4_rz[p2][:], start=True, stop=True),
                reads=[("rz", par), ("ones_row",), ("psf", orz)],
                writes=[("ps", orz), ("rz_f", par)], inc=ct)
            kb.op("vector", lambda e, o2=ow: e.tensor_copy(s4_of[:], ps[o2][0:64, :]),
                  reads=[("ps", ow), ("s4of_f",)], writes=[("s4of",)], inc=cv)
            kb.op("vector", lambda e, o2=orz: e.tensor_mul(
                s4_ob[:], s4_of[:], ps[o2][0:64, :]),
                reads=[("s4of",), ("ps", orz), ("s4ob_f",)],
                writes=[("s4ob",), ("psf", ow), ("psf", orz), ("s4of_f",)], inc=cv)
            kb.op("sync", lambda e, h2=h, q2=sq2, w2=w: e.dma_start(
                out=oT_d[h2 * 64:(h2 + 1) * 64,
                         q2 * T + w2 * 512:q2 * T + (w2 + 1) * 512],
                in_=s4_ob[:]),
                reads=[("s4ob",)], writes=[("oT", sq2, w, h), ("s4ob_f",)],
                inc=kb.dsem("s4obst"), n=16)

        pend1 = pend2 = None
        for idx, (h, w) in enumerate([(h, w) for h in range(NH) for w in range(NW)]):
            cur = (h, w, idx % 2)
            for kc in range(KC):
                score_one(cur, kc)
                if pend1 is not None:
                    attnv_one(pend1, kc)
            if pend1 is not None:
                for kc in range(KC):
                    kb.mark(("probs_f", pend1[2], kc), ct)
            if pend2 is not None:
                norm_b(pend2, sq_)
            if pend1 is not None:
                norm_a(pend1)
            pend2, pend1 = pend1, cur
        for kc in range(KC):
            attnv_one(pend1, kc)
        for kc in range(KC):
            kb.mark(("probs_f", pend1[2], kc), ct)
        norm_b(pend2, sq_)
        norm_a(pend1)
        norm_b(pend1, sq_)
        kb.mark(("s4qf_f",), ct)
        kb.mark(("s4kf_f",), ct)
        for vi in range(KC // 4):
            kb.mark(("s4vaug_f", vi), ct)

    kb.flush()
    sB.close()
    kb.retire()

    sC = Scope(nc)
    bc_row_tile[0] = sC.sb("bcrowC", [1, D], F32)
    g2b, g2b_k = bcast_row(sC, "g2b", g2_in)
    be2b, be2b_k = bcast_row(sC, "be2b", be2_in)
    gw_sb = sC.sb("gw_sb", [128, DC * E], F32)
    gbb_sb = sC.sb("gbb_sb", [128, E], F32)
    kb.op("sync", lambda e: e.dma_start(
        out=gw_sb[:].rearrange("p (c e) -> p c e", e=E),
        in_=gw_in.rearrange("(c p) e -> p c e", p=128)),
        writes=[("gw",)], inc=kb.dsem("gw"), n=16)
    kb.op("sync", lambda e: e.dma_start(out=bc_row_tile[0][:, 0:E], in_=gb_in[None, :]),
          reads=[("bcrow_f",)], writes=[("bcrowg",)], inc=kb.dsem("bcrow"), n=16)
    kb.op("tensor", lambda e: e.matmul(
        ps[0][:, 0:E], ones_row[:], bc_row_tile[0][:, 0:E], start=True, stop=True),
        reads=[("bcrowg",), ("ones_row",), ("psf", 0)], writes=[("ps", 0)], inc=ct)
    kb.op("vector", lambda e: e.tensor_copy(gbb_sb[:], ps[0][:, 0:E]),
          reads=[("ps", 0)], writes=[("gbb",), ("psf", 0)], inc=cv)

    s5_wo = sC.sb("s5wo", [128, DC * D], BF16)
    kb.op("gpsimd", lambda e: e.dma_start(
        out=s5_wo[:].rearrange("p (k n) -> p k n", k=DC),
        in_=wo_in.rearrange("(k p) n -> p k n", p=128)),
        writes=[("s5wo",)], inc=kb.dsem("s5wo"), n=16)

    s5_oT = [sC.sb(f"s5oT{i}", [128, DC * 512], BF16) for i in range(2)]
    s5_xnbo = [sC.sb(f"s5xnbo{i}", [128, 4 * D], F32) for i in range(2)]
    s5_xp = sC.sb("s5xp", [128, 4 * D], F32)
    s5_hn2 = sC.sb("s5hn2", [128, 4 * D], F32)
    s5_hn2T = [sC.sb(f"s5hn2T{l}", [128, DC * 128], F32) for l in range(2)]
    s5_lg = [sC.sb(f"s5lg{l}", [128, E], F32) for l in range(2)]
    s5_mx = [sC.sb(f"s5mx{l}", [128, 1], F32) for l in range(2)]
    s5_raw = [sC.sb(f"s5raw{l}", [128, E], F32) for l in range(2)]
    s5_cnt = [sC.sb(f"s5cnt{l}", [128, E], F32) for l in range(2)]
    s5_msk = [sC.sb(f"s5msk{l}", [128, E], F32) for l in range(2)]
    s5_mT = [sC.sb(f"s5mT{l}", [8, 128], F32) for l in range(2)]
    T5B = [(2, 3), (0, 1)]
    GB = [4, 6]
    MB = [5, 7]

    def p2_transpose(tc):
        ln = tc % 2
        for c in range(DC):
            tb5 = T5B[ln][c // 4]
            last4 = (c % 4 == 3)
            kb.op("tensor", lambda e, t2=tc, c2=c, b2=tb5: e.matmul(
                ps[b2][:, (c2 % 4) * 128:(c2 % 4 + 1) * 128],
                s5_hn2[:, t2 * D + c2 * 128:t2 * D + (c2 + 1) * 128],
                id_sb[:], is_transpose=True),
                reads=[("s5hn2_c", tc), ("id",), ("psf", tb5)],
                writes=[("ps", tb5)] if last4 else [],
                inc=ct if last4 else None)

    def p2_evac(tc):
        ln = tc % 2
        for hf in range(2):
            tb5 = T5B[ln][hf]
            kb.op("vector", lambda e, b2=tb5, h2=hf, l2=ln: e.tensor_copy(
                s5_hn2T[l2][:, h2 * 512:(h2 + 1) * 512], ps[b2][:]),
                reads=[("ps", tb5), ("hn2T_f", ln)],
                writes=[("hn2T", ln, hf), ("psf", tb5)], inc=cv)

    def p2_gate(tc):
        ln = tc % 2
        gb = GB[ln]
        for k in range(DC):
            kb.op("tensor", lambda e, k2=k, l2=ln, g2=gb: e.matmul(
                ps[g2][:, 0:E], s5_hn2T[l2][:, k2 * 128:(k2 + 1) * 128],
                gw_sb[:].rearrange("p (c e) -> p c e", e=E)[:, k2, :],
                start=(k2 == 0), stop=(k2 == DC - 1)),
                reads=[("hn2T", ln, 0), ("hn2T", ln, 1), ("gw",), ("psf", gb)],
                writes=[("ps", gb)] if k == DC - 1 else [],
                inc=ct if k == DC - 1 else None)
        kb.mark(("hn2T_f", ln), ct)

    def p2_mask(tc):
        ln = tc % 2
        gb = GB[ln]
        kb.op("vector", lambda e, l2=ln, g2=gb: e.tensor_add(
            s5_lg[l2][:], ps[g2][:, 0:E], gbb_sb[:]),
            reads=[("ps", gb), ("gbb",), ("s5lg_f", ln)],
            writes=[("s5lg", ln), ("psf", gb)], inc=cv)
        kb.op("vector", lambda e, l2=ln: e.reduce_max(
            s5_mx[l2][:], s5_lg[l2][:], axis=AX.X),
            reads=[("s5lg", ln)], writes=[("s5mx", ln)], inc=cv)
        kb.op("vector", lambda e, l2=ln: e.scalar_tensor_tensor(
            s5_raw[l2][:], s5_lg[l2][:], s5_mx[l2][:], ones8[:],
            ALU.is_equal, ALU.mult),
            reads=[("s5mx", ln), ("ones8",)],
            writes=[("s5raw", ln), ("s5lg_f", ln)], inc=cv)
        kb.op("vector", lambda e, l2=ln: e.tensor_tensor_scan(
            s5_cnt[l2][:], ones8[:], s5_raw[l2][:], 0.0, ALU.mult, ALU.add),
            reads=[("s5raw", ln), ("ones8",)], writes=[("s5cnt", ln)], inc=cv)
        kb.op("vector", lambda e, l2=ln: e.tensor_single_scalar(
            s5_cnt[l2][:], s5_cnt[l2][:], 1.5, ALU.is_le),
            reads=[("s5cnt", ln)], writes=[("s5cnt2", ln)], inc=cv)
        kb.op("vector", lambda e, l2=ln: e.tensor_mul(
            s5_msk[l2][:], s5_raw[l2][:], s5_cnt[l2][:]),
            reads=[("s5cnt2", ln), ("s5raw", ln), ("s5msk_f", ln), ("s5msk_f2", ln)],
            writes=[("s5msk", ln)], inc=cv)

    def p2_store(tc, chunk):
        ln = tc % 2
        mb = MB[ln]
        kb.op("gpsimd", lambda e, c2=chunk, l2=ln: e.dma_start(
            out=mask_d[c2], in_=s5_msk[l2][:]),
            reads=[("s5msk", ln)], writes=[("mask", chunk), ("s5msk_f2", ln)],
            inc=kb.dsem(f"s5mskst{ln}"), n=16)
        kb.op("tensor", lambda e, l2=ln, m2=mb: e.matmul(
            ps[m2][0:8, 0:128], s5_msk[l2][:], id_sb[:], is_transpose=True),
            reads=[("s5msk", ln), ("id",), ("psf", mb)], writes=[("ps", mb)], inc=ct)
        kb.op("vector", lambda e, l2=ln, m2=mb: e.tensor_copy(
            s5_mT[l2][:], ps[m2][0:8, 0:128]),
            reads=[("ps", mb), ("s5mT_f", ln)], writes=[("s5mT", ln), ("psf", mb)],
            inc=cv)
        kb.op("gpsimd", lambda e, c2=chunk, l2=ln: e.dma_start(
            out=maskT_d[:, c2 * 128:(c2 + 1) * 128], in_=s5_mT[l2][:]),
            reads=[("s5mT", ln)],
            writes=[("maskT", chunk), ("s5mT_f", ln), ("s5msk_f", ln)],
            inc=kb.dsem(f"s5mTst{ln}"), n=16)
    prev_tile = [None]
    for i in range(NT2):
        sl = i % 2
        t0 = i * 512
        sq_of = t0 // T
        w_of = (t0 % T) // 512
        kb.op("sync", lambda e, i2=i, s2=sl: e.dma_start(
            out=s5_oT[s2][:].rearrange("p (c t) -> p c t", c=DC),
            in_=oT_d[:, i2 * 512:(i2 + 1) * 512].rearrange("(c p) t -> p c t", p=128)),
            reads=[("s5oT_f", sl)] + [("oT", sq_of, w_of, h) for h in range(NH)],
            writes=[("s5oT", sl)], inc=kb.dsem(f"s5oT{sl}"), n=16)
        kb.op("sync", lambda e, i2=i, s2=sl: e.dma_start(
            out=s5_xnbo[s2][:].rearrange("p (c d) -> p c d", c=4),
            in_=xnbo_d[i2 * 512:(i2 + 1) * 512, :].rearrange("(c p) d -> p c d", p=128)),
            reads=[("s5xnbo_f", sl), ("xnbo", i)],
            writes=[("s5xnbo", sl)], inc=kb.dsem(f"s5xnbo{sl}"), n=16)

        for tc in range(4):
            for nhf in range(2):
                bb = nhf
                for k in range(DC):
                    kb.op("tensor", lambda e, t2=tc, n2=nhf, k2=k, b2=bb, s2=sl: e.matmul(
                        ps[b2][:],
                        s5_oT[s2][:, k2 * 512 + t2 * 128:k2 * 512 + (t2 + 1) * 128],
                        s5_wo[:, k2 * D + n2 * 512:k2 * D + (n2 + 1) * 512],
                        start=(k2 == 0), stop=(k2 == DC - 1)),
                        reads=[("s5wo",), ("s5oT", sl), ("psf", bb)],
                        writes=[("ps", bb)] if k == DC - 1 else [],
                        inc=ct if k == DC - 1 else None)
                extra = [("xp", prev_tile[0])] if prev_tile[0] is not None else []
                kb.op("vector", lambda e, t2=tc, n2=nhf, b2=bb, s2=sl: e.tensor_add(
                    s5_xp[:, t2 * D + n2 * 512:t2 * D + (n2 + 1) * 512], ps[b2][:],
                    s5_xnbo[s2][:, t2 * D + n2 * 512:t2 * D + (n2 + 1) * 512]),
                    reads=[("ps", bb), ("s5xnbo", sl), ("s5xp_f",)] + extra,
                    writes=[("s5xp",), ("s5xph", tc, nhf), ("psf", bb)], inc=cv)
        kb.op("gpsimd", lambda e, i2=i: e.dma_start(
            out=xp_d[i2 * 512:(i2 + 1) * 512, :].rearrange("(t p) d -> p t d", p=128),
            in_=s5_xp[:].rearrange("p (t d) -> p t d", t=4)),
            reads=[("s5xph", t2, n2) for t2 in range(4) for n2 in range(2)],
            writes=[("xp", i)], inc=kb.dsem("s5xpst"), n=16)
        extra_h = [("hn2b", prev_tile[0])] if prev_tile[0] is not None else []
        emit_ln4(s5_xp[:], [("s5xph", t2, n2) for t2 in range(4) for n2 in range(2)],
                 s5_hn2[:], ("s5ln",), extra_apply_reads=extra_h + [("s5hn2_f",)])
        kb.mark(("s5xp_f",), cs)
        for c in range(4):
            kb.op("vector", lambda e, c2=c: e.tensor_mul(
                s5_hn2[:, c2 * D:(c2 + 1) * D], s5_hn2[:, c2 * D:(c2 + 1) * D], g2b[:]),
                reads=[("s5ln_c", c)] + g2b_k, writes=[("s5hg", c)], inc=cv)
            kb.op("vector", lambda e, c2=c: e.tensor_add(
                s5_hn2[:, c2 * D:(c2 + 1) * D], s5_hn2[:, c2 * D:(c2 + 1) * D], be2b[:]),
                reads=[("s5hg", c)] + be2b_k,
                writes=[("s5hn2",), ("s5hn2_c", c)], inc=cv)
        kb.op("gpsimd", lambda e, i2=i: e.dma_start(
            out=hn2_d[i2 * 512:(i2 + 1) * 512, :].rearrange("(t p) d -> p t d", p=128),
            in_=s5_hn2[:].rearrange("p (t d) -> p t d", t=4)),
            reads=[("s5hn2",)] + [("s5hn2_c", c) for c in range(4)],
            writes=[("hn2b", i)], inc=kb.dsem("s5hn2st"), n=16)

        for tp in range(2):
            tca, tcb = 2 * tp, 2 * tp + 1
            p2_transpose(tca); p2_transpose(tcb)
            p2_evac(tca); p2_evac(tcb)
            p2_gate(tca); p2_gate(tcb)
            p2_mask(tca); p2_mask(tcb)
            p2_store(tca, t0 // 128 + tca); p2_store(tcb, t0 // 128 + tcb)
        prev_tile[0] = i
        kb.mark(("s5hn2_f",), ct)
        kb.mark(("s5oT_f", sl), ct)
        kb.mark(("s5xnbo_f", sl), cv)

    kb.flush()
    sC.close()
    kb.retire()

    _bregs = {}

    def breg(e, v):
        if v not in _bregs:
            _bregs[v] = e.to_reg(v)
        return _bregs[v]

    sD1 = Scope(nc)
    iota_sb = sD1.sb("iota_sb", [128, 1], F32)
    teb = sD1.sb("teb", [128, NTS], F32)
    wxf = sD1.sb("wxf", [128, NTS], F32)
    widx1 = sD1.sb("widx1", [128, NTS], I32)
    widx2 = sD1.sb("widx2", [128, NTS], I32)
    b2e_sb = sD1.sb("b2e_sb", [8, D], F32)
    kb.op("sync", lambda e: e.dma_start(out=iota_sb[:], in_=iota_in),
          writes=[("iotac",)], inc=kb.dsem("iota"), n=16)
    kb.op("sync", lambda e: e.dma_start(out=b2e_sb[:], in_=b2_in),
          writes=[("b2e",)], inc=kb.dsem("b2e"), n=16)
    for e_ in range(E):
        kb.op("sync", lambda e, e2=e_: e.dma_start(
            out=bias_stg[0:32, :], in_=b1_in[e2].rearrange("(m p) -> m p", p=128)),
            reads=[("bstg_f",)], writes=[("bstg",)], inc=kb.dsem("bstg"), n=16)
        kb.op("tensor", lambda e: e.matmul(
            ps[7][:, 0:32], bias_stg[0:32, :], id_sb[0:32, 0:32], is_transpose=True),
            reads=[("bstg",), ("id",), ("psf", 7)], writes=[("ps", 7)], inc=ct)
        kb.op("vector", lambda e: e.tensor_copy(ln_tmp["sq"][:, 0:32], ps[7][:, 0:32]),
              reads=[("ps", 7), ("lnsq_f",)], writes=[("b1row",), ("psf", 7), ("bstg_f",)],
              inc=cv)
        kb.op("gpsimd", lambda e, e2=e_: e.dma_start(
            out=b1s_d[e2], in_=ln_tmp["sq"][:, 0:32]),
            reads=[("b1row",)], writes=[("b1s", e_), ("lnsq_f",)], inc=kb.dsem("b1sst"), n=16)
    kb.mark(("b1s_all",), kb.dsem("b1sst"))

    sR = Scope(nc)
    mT8 = sR.sb("mT8", [8, TOK], F32)
    on8 = sR.sb("on8", [8, TOK], F32)
    cum8 = sR.sb("cum8", [8, TOK], F32)
    rankr = sR.sb("rankr", [1, TOK], F32)
    ranki = sR.sb("ranki", [1, TOK], I32)
    nt8 = sR.sb("nt8", [8, 1], F32)
    nti = sR.sb("nti", [8, 1], I32)
    ntr = sR.sb("ntr", [1, 8], F32)
    cumr = sR.sb("cumr", [1, 8], F32)
    offr = sR.sb("offr", [1, 8], F32)
    offc = sR.sb("offc", [8, 1], F32)
    inclc = sR.sb("inclc", [8, 1], F32)
    ior = sR.sb("ior", [1, NTS], F32)
    i8r = sR.sb("i8r", [8, NTS], F32)
    ger = sR.sb("ger", [8, NTS], F32)
    ter = sR.sb("ter", [1, NTS], F32)
    onn = sR.sb("onn", [128, NTS], F32)

    kb.op("sync", lambda e: e.dma_start(out=mT8[:], in_=maskT_d),
          reads=[("maskT", ch) for ch in range(TOK // 128)],
          writes=[("rt_m",)], inc=kb.dsem("mT8"), n=16)
    kb.op("vector", lambda e: e.memset(on8[:], 1.0), writes=[("rt_on",)], inc=cv)
    kb.op("vector", lambda e: e.memset(onn[:], 1.0), writes=[("rt_onn",)], inc=cv)
    kb.op("vector", lambda e: e.reduce_sum(nt8[:], mT8[:], axis=AX.X),
          reads=[("rt_m",)], writes=[("rt_cnt",)], inc=cv)
    kb.op("vector", lambda e: e.tensor_tensor_scan(
        cum8[:], on8[:], mT8[:], 0.0, ALU.mult, ALU.add),
        reads=[("rt_m",), ("rt_on",)], writes=[("rt_cum",)], inc=cv)
    kb.op("vector", lambda e: e.tensor_scalar(
        nt8[:], nt8[:], 511.0, 1.0 / 512.0, ALU.add, ALU.mult),
        reads=[("rt_cnt",)], writes=[("rt_nt1",)], inc=cv)
    kb.op("vector", lambda e: e.tensor_single_scalar(nt8[:], nt8[:], -0.4995, ALU.add),
          reads=[("rt_nt1",)], writes=[("rt_nt2",)], inc=cv)
    kb.op("vector", lambda e: e.tensor_copy(nti[:], nt8[:]),
          reads=[("rt_nt2",)], writes=[("rt_nti",)], inc=cv)
    kb.op("vector", lambda e: e.tensor_copy(nt8[:], nti[:]),
          reads=[("rt_nti",)], writes=[("rt_ntf",)], inc=cv)
    kb.op("tensor", lambda e: e.matmul(
        ps[7][0:1, 0:8], nt8[:], id_sb[0:8, 0:8], is_transpose=True),
        reads=[("rt_ntf",), ("id",), ("psf", 7)], writes=[("ps", 7)], inc=ct)
    kb.op("vector", lambda e: e.tensor_copy(ntr[:], ps[7][0:1, 0:8]),
          reads=[("ps", 7)], writes=[("rt_ntr",), ("psf", 7)], inc=cv)
    kb.op("vector", lambda e: e.tensor_tensor_scan(
        cumr[:], ones_row[0:1, 0:8], ntr[:], 0.0, ALU.mult, ALU.add),
        reads=[("rt_ntr",), ("ones_row",)], writes=[("rt_cumr",)], inc=cv)
    kb.op("vector", lambda e: e.tensor_sub(offr[:], cumr[:], ntr[:]),
          reads=[("rt_cumr",), ("rt_ntr",)], writes=[("rt_offr",)], inc=cv)
    kb.op("vector", lambda e: e.tensor_scalar(
        offr[:], offr[:], 512.0, -1.0, ALU.mult, ALU.add),
        reads=[("rt_offr",)], writes=[("rt_offm1",)], inc=cv)
    kb.op("tensor", lambda e: e.matmul(
        ps[7][0:8, 0:1], offr[:], id_sb[0:1, 0:1], is_transpose=True),
        reads=[("rt_offm1",), ("id",), ("psf", 7)], writes=[("ps", 7)], inc=ct)
    kb.op("vector", lambda e: e.tensor_copy(offc[:], ps[7][0:8, 0:1]),
          reads=[("ps", 7)], writes=[("rt_offc",), ("psf", 7)], inc=cv)
    kb.op("tensor", lambda e: e.matmul(
        ps[6][0:8, 0:1], cumr[:], id_sb[0:1, 0:1], is_transpose=True),
        reads=[("rt_cumr",), ("id",), ("psf", 6)], writes=[("ps", 6)], inc=ct)
    kb.op("vector", lambda e: e.tensor_copy(inclc[:], ps[6][0:8, 0:1]),
          reads=[("ps", 6)], writes=[("rt_inclc",), ("psf", 6)], inc=cv)
    kb.op("vector", lambda e: e.scalar_tensor_tensor(
        cum8[:], cum8[:], offc[:], mT8[:], ALU.add, ALU.mult),
        reads=[("rt_cum",), ("rt_offc",), ("rt_m",)], writes=[("rt_rk8",)], inc=cv)
    for c in range(TOK // 512):
        bb = 4 + c % 2
        kb.op("tensor", lambda e, c2=c, b2=bb: e.matmul(
            ps[b2][0:1, :], on8[:, 0:1], cum8[:, c2 * 512:(c2 + 1) * 512],
            start=True, stop=True),
            reads=[("rt_rk8",), ("rt_on",), ("psf", bb)], writes=[("ps", bb)], inc=ct)
        kb.op("vector", lambda e, c2=c, b2=bb: e.tensor_copy(
            rankr[:, c2 * 512:(c2 + 1) * 512], ps[b2][0:1, :]),
            reads=[("ps", bb)], writes=[("rt_rkr",), ("psf", bb)], inc=cv)
    kb.op("vector", lambda e: e.tensor_copy(ranki[:], rankr[:]),
          reads=[("rt_rkr",)], writes=[("rt_rki",)], inc=cv)
    kb.op("gpsimd", lambda e: e.dma_start(
        out=rank_d.rearrange("(a t) -> a t", a=1), in_=ranki[:]),
        reads=[("rt_rki",)], writes=[("rankd",)], inc=kb.dsem("rankst"), n=16)
    kb.op("vector", lambda e: e.tensor_tensor_scan(
        ior[:], ones_row[0:1, 0:NTS], ones_row[0:1, 0:NTS], 0.0, ALU.mult, ALU.add),
        reads=[("ones_row",)], writes=[("rt_io1",)], inc=cv)
    kb.op("vector", lambda e: e.tensor_single_scalar(ior[:], ior[:], -1.0, ALU.add),
          reads=[("rt_io1",)], writes=[("rt_io",)], inc=cv)
    kb.op("tensor", lambda e: e.matmul(
        ps[4][0:8, 0:NTS], ones_row[0:1, 0:8], ior[:], start=True, stop=True),
        reads=[("rt_io",), ("ones_row",), ("psf", 4)], writes=[("ps", 4)], inc=ct)
    kb.op("vector", lambda e: e.tensor_copy(i8r[:], ps[4][0:8, 0:NTS]),
          reads=[("ps", 4)], writes=[("rt_i8",), ("psf", 4)], inc=cv)
    kb.op("vector", lambda e: e.scalar_tensor_tensor(
        ger[:], i8r[:], inclc[:], on8[:, 0:NTS], ALU.is_ge, ALU.mult),
        reads=[("rt_i8",), ("rt_inclc",), ("rt_on",)], writes=[("rt_ge",)], inc=cv)
    kb.op("tensor", lambda e: e.matmul(
        ps[5][0:1, 0:NTS], on8[:, 0:1], ger[:], start=True, stop=True),
        reads=[("rt_ge",), ("rt_on",), ("psf", 5)], writes=[("ps", 5)], inc=ct)
    kb.op("vector", lambda e: e.tensor_copy(ter[:], ps[5][0:1, 0:NTS]),
          reads=[("ps", 5)], writes=[("rt_te",), ("psf", 5)], inc=cv)
    kb.op("tensor", lambda e: e.matmul(
        ps[6][:, 0:NTS], ones_row[:], ter[:], start=True, stop=True),
        reads=[("rt_te",), ("ones_row",), ("psf", 6)], writes=[("ps", 6)], inc=ct)
    kb.op("vector", lambda e: e.tensor_copy(teb[:], ps[6][:, 0:NTS]),
          reads=[("ps", 6)], writes=[("rt_teb",), ("psf", 6)], inc=cv)
    kb.op("vector", lambda e: e.tensor_single_scalar(wxf[:], teb[:], 1024.0, ALU.mult),
          reads=[("rt_teb",)], writes=[("rt_w1a",)], inc=cv)
    kb.op("vector", lambda e: e.tensor_single_scalar(
        ln_tmp["sq"][:, 0:1], iota_sb[:], 8.0, ALU.mult),
        reads=[("iotac",), ("lnsq_f",)], writes=[("rt_i32s",)], inc=cv)
    kb.op("vector", lambda e: e.scalar_tensor_tensor(
        wxf[:], wxf[:], ln_tmp["sq"][:, 0:1], onn[:], ALU.add, ALU.mult),
        reads=[("rt_w1a",), ("rt_i32s",), ("rt_onn",)], writes=[("rt_w1b",)], inc=cv)
    kb.op("vector", lambda e: e.tensor_copy(widx1[:], wxf[:]),
          reads=[("rt_w1b",)], writes=[("widx1",)], inc=cv)
    kb.op("vector", lambda e: e.tensor_single_scalar(wxf[:], teb[:], 128.0, ALU.mult),
          reads=[("rt_teb",), ("widx1",)], writes=[("rt_w2a",)], inc=cv)
    kb.op("vector", lambda e: e.scalar_tensor_tensor(
        wxf[:], wxf[:], iota_sb[:], onn[:], ALU.add, ALU.mult),
        reads=[("rt_w2a",), ("iotac",), ("rt_onn",)], writes=[("rt_w2b",)], inc=cv)
    kb.op("vector", lambda e: e.tensor_copy(widx2[:], wxf[:]),
          reads=[("rt_w2b",)], writes=[("widx2",)], inc=cv)
    kb.mark(("lnsq_f",), cv)

    kb.flush()
    sR.close()
    kb.retire()

    sD = Scope(nc)
    sc_h = [sD.sb(f"sc_h{i}", [128, D], BF16) for i in range(2)]
    sc_i = [sD.sb(f"sc_i{i}", [128, 1], I32) for i in range(2)]
    s6_hn2T = [sD.sb(f"s6hn2T{i}", [128, DC * TT6], BF16) for i in range(2)]
    s6_w1 = [sD.sb(f"s6w1_{i}", [128, 4 * DC * 128], BF16) for i in range(2)]
    s6_w2 = sD.sb("s6w2", [128, FC * D], BF16)
    s6_b1 = sD.sb("s6b1", [128, FC], F32)
    s6_mid = sD.sb("s6mid", [128, FC * TT6], BF16)
    s6_ys = [sD.sb(f"s6ys{i}", [128, D], F32) for i in range(2)]
    s6_yg = [sD.sb(f"s6yg{i}", [128, D], F32) for i in range(2)]
    s6_gi = [sD.sb(f"s6gi{i}", [128, 1], I32) for i in range(2)]
    s6_mT = sD.sb("s6mT", [8, TT6], F32)
    s6_xp = [sD.sb(f"s6xp{i}", [128, D], F32) for i in range(2)]
    s6_x = [sD.sb(f"s6x{i}", [128, D], F32) for i in range(2)]
    s6_s = sD.sb("s6s", [128, DC * TT6], F32)
    s6_cmb = sD.sb("s6cmb", [128, D], F32)
    s6_g = sD.sb("s6g", [128, D], F32)
    s6_out = sD.sb("s6out", [128, D], F32)

    for c in range(TOK // 128):
        sl = c % 2
        kb.op("scalar", lambda e, c2=c, s9=sl: e.dma_start(
            out=sc_h[s9][:], in_=hn2_d[c2 * 128:(c2 + 1) * 128, :]),
            reads=[("sc_h_f", sl), ("hn2b", c // 4)], writes=[("sc_h", sl)], inc=kb.dsem(f"sch{sl}"), n=16)
        kb.op("scalar", lambda e, c2=c, s9=sl: e.dma_start(
            out=sc_i[s9][:],
            in_=rank_d.rearrange("(c p) -> c p", p=128)[c2][:, None]),
            reads=[("sc_i_f", sl), ("rankd",)], writes=[("sc_i", sl)], inc=kb.dsem(f"sci{sl}"), n=16)
        kb.op("gpsimd", lambda e, s9=sl: e.indirect_dma_start(
            out=hn2s_d[0:128, :], out_offset=IndirectOffsetOnAxis(sc_i[s9][:], 0),
            in_=sc_h[s9][:], in_offset=None,
            bounds_check=breg(e, SLOTS - 1), oob_is_err=False),
            reads=[("sc_h", sl), ("sc_i", sl)],
            writes=[("hn2s_s", sl), ("sc_h_f", sl), ("sc_i_f", sl)], inc=kb.dsem(f"scat{sl}"), n=16)

    for j in range(NTS):
        st6 = j % 2
        for c in range(DC):
            kb.op("scalar", lambda e, j2=j, c2=c, s9=st6: e.dma_start(
                out=s6_hn2T[s9][:, c2 * TT6:(c2 + 1) * TT6],
                in_=hn2s_d[j2 * 512:(j2 + 1) * 512, c2 * 128:(c2 + 1) * 128],
                transpose=True),
                reads=[("s6hn2T_f", st6), ("hn2s_s", 0), ("hn2s_s", 1)],
                writes=[("s6hn2T", st6, c)], inc=kb.dsem(f"s6hn2T{st6}"), n=16)
        hn2T_keys = [("s6hn2T", st6, c) for c in range(DC)]
        kb.op("gpsimd", lambda e, j2=j: e.indirect_dma_start(
            out=s6_w2[:], out_offset=None,
            in_=w2s_d.rearrange("e p n -> (e p) n")[0:128, :],
            in_offset=IndirectOffsetOnAxis(widx2[:, j2:j2 + 1], 0),
            bounds_check=breg(e, E * 128 - 1), oob_is_err=False),
            reads=[("w1s_done", 0), ("w1s_done", 1), ("widx2",), ("s6w2_f",)],
            writes=[("s6w2",)], inc=kb.dsem("s6w2"), n=16)
        kb.op("gpsimd", lambda e, j2=j: e.indirect_dma_start(
            out=s6_b1[:], out_offset=None,
            in_=b1s_d.rearrange("e p m -> (e p) m")[0:128, :],
            in_offset=IndirectOffsetOnAxis(widx2[:, j2:j2 + 1], 0),
            bounds_check=breg(e, E * 128 - 1), oob_is_err=False),
            reads=[("b1s_all",), ("widx2",), ("s6b1_f",)],
            writes=[("s6b1",)], inc=kb.dsem("s6b1"), n=16)
        for mg in range(FC // 4):
            sl = mg % 2
            kb.op("gpsimd", lambda e, j2=j, m2=mg, s9=sl: e.indirect_dma_start(
                out=s6_w1[s9][:], out_offset=None,
                in_=w1s_d.rearrange("e p m n -> (e p) (m n)")[0:128, 0:4 * 1024],
                in_offset=IndirectOffsetOnAxis(widx1[:, j2:j2 + 1], 0),
                element_offset=m2 * 4 * 1024,
                bounds_check=breg(e, E * 128 * FC // 4 - 1), oob_is_err=False),
                reads=[("w1s_done", 0), ("w1s_done", 1), ("widx1",), ("s6w1_f", sl)],
                writes=[("s6w1", sl)], inc=kb.dsem(f"s6w1_{sl}"), n=16)
            for mm in range(4):
                m = mg * 4 + mm
                bu = m % 2
                for k in range(DC):
                    last = (k == DC - 1)
                    rel = last and (mm == 3)
                    kb.op("tensor", lambda en, m3=mm, k2=k, b2=bu, s9=sl, s8=st6: en.matmul(
                        ps[b2][:],
                        s6_w1[s9][:, m3 * 1024 + k2 * 128:m3 * 1024 + (k2 + 1) * 128],
                        s6_hn2T[s8][:, k2 * TT6:(k2 + 1) * TT6],
                        start=(k2 == 0), stop=last),
                        reads=[("s6w1", sl), ("psf", bu)] + hn2T_keys,
                        writes=([("ps", bu)] if last else []) +
                               ([("s6w1_f", sl)] if rel else []),
                        inc=ct if last else None)
                kb.op("scalar", lambda en, m2=m, b2=bu: en.activation(
                    s6_mid[:, m2 * TT6:(m2 + 1) * TT6], ps[b2][:], AF.Relu,
                    bias=s6_b1[:, m2:m2 + 1]),
                    reads=[("ps", bu), ("s6b1",), ("mid_f", m)],
                    writes=[("mid", m), ("psf", bu)], inc=cs)
        mid_keys = [("mid", m) for m in range(FC)]
        for tc in range(NTC6):
            for nhf in range(2):
                bd = 2 + (tc * 2 + nhf) % 2
                for k in range(FC):
                    last = (k == FC - 1)
                    kb.op("tensor", lambda en, t2=tc, n2=nhf, k2=k, b2=bd: en.matmul(
                        ps[b2][:],
                        s6_mid[:, k2 * TT6 + t2 * 128:k2 * TT6 + (t2 + 1) * 128],
                        s6_w2[:, k2 * D + n2 * 512:k2 * D + (n2 + 1) * 512],
                        start=(k2 == 0), stop=last),
                        reads=[("psf", bd), ("s6w2",)] + mid_keys,
                        writes=[("ps", bd)] if last else [],
                        inc=ct if last else None)
                kb.op("vector", lambda en, t2=tc, n2=nhf, b2=bd: en.tensor_copy(
                    s6_ys[t2 % 2][:, n2 * 512:(n2 + 1) * 512], ps[b2][:]),
                    reads=[("ps", bd), ("s6ys_f", tc % 2)],
                    writes=[("s6ysh", tc % 2, nhf), ("psf", bd)], inc=cv)
            kb.op("scalar", lambda en, j2=j, t2=tc: en.dma_start(
                out=ys_d[j2 * 512 + t2 * 128:j2 * 512 + (t2 + 1) * 128, :],
                in_=s6_ys[t2 % 2][:]),
                reads=[("s6ysh", tc % 2, 0), ("s6ysh", tc % 2, 1)],
                writes=[("ys", j, tc), ("s6ys_f", tc % 2)], inc=kb.dsem(f"ys{tc % 2}"), n=16)
        for m in range(FC):
            kb.mark(("mid_f", m), ct)
        kb.mark(("s6w2_f",), ct)
        kb.mark(("s6b1_f",), cs)
        kb.mark(("s6hn2T_f", st6), ct)

    ys_keys = [("ys", j, tc) for j in range(NTS) for tc in range(NTC6)]

    for i in range(NT6):
        t0 = i * TT6
        kb.op("scalar", lambda e, i2=i: e.dma_start(
            out=s6_mT[:], in_=maskT_d[:, i2 * TT6:(i2 + 1) * TT6]),
            reads=[("s6mT_f",)] + [("maskT", t0 // 128 + jj) for jj in range(NTC6)],
            writes=[("s6mT",)], inc=kb.dsem("s6mTl"), n=16)
        kb.op("scalar", lambda e, i2=i: e.dma_start(
            out=s6_s[:].rearrange("p (c t) -> p c t", c=DC),
            in_=s_d[:, i2 * TT6:(i2 + 1) * TT6].rearrange("(c p) t -> p c t", p=128)),
            reads=[("s6s_f",)] + [("s_d", m, t0 // T) for m in range(DC)],
            writes=[("s6s",)], inc=kb.dsem("s6sl"), n=16)
        for tc in range(NTC6):
            chunk = t0 // 128 + tc
            slc = tc % 2
            kb.op("scalar", lambda e, c2=chunk, s9=slc: e.dma_start(
                out=s6_gi[s9][:],
                in_=rank_d.rearrange("(c p) -> c p", p=128)[c2][:, None]),
                reads=[("s6gi_f", slc), ("rankd",)], writes=[("s6gi", slc)], inc=kb.dsem(f"s6gi{slc}"), n=16)
            kb.op("gpsimd", lambda e, s9=slc: e.indirect_dma_start(
                out=s6_yg[s9][:], out_offset=None,
                in_=ys_d[0:128, :], in_offset=IndirectOffsetOnAxis(s6_gi[s9][:], 0),
                bounds_check=breg(e, SLOTS - 1), oob_is_err=False),
                reads=[("s6gi", slc), ("s6yg_f", slc)] + ys_keys,
                writes=[("s6yg", slc)], inc=kb.dsem(f"s6yg{slc}"), n=16)
            if chk_yg is not None and i == 0 and tc == 0:
                kb.op("gpsimd", lambda e: e.dma_start(out=chk_yg, in_=s6_yg[0][:]),
                      reads=[("s6yg", 0)], writes=[("chkyg",)], inc=kb.dsem("chk"), n=16)
                kb.op("gpsimd", lambda e: e.dma_start(out=chk_gi, in_=s6_gi[0][:]),
                      reads=[("s6gi", 0)], writes=[("chkgi",)], inc=kb.dsem("chk"), n=16)
                kb.op("gpsimd", lambda e: e.dma_start(
                    out=chk_rank.rearrange("(a t) -> a t", a=1),
                    in_=rank_d.rearrange("(a t) -> a t", a=1)),
                    reads=[("rankd",)], writes=[("chkrk",)], inc=kb.dsem("chk"), n=16)
            kb.op("scalar", lambda e, c2=chunk, s9=slc: e.dma_start(
                out=s6_xp[s9][:], in_=xp_d[c2 * 128:(c2 + 1) * 128, :]),
                reads=[("s6xp_f", slc), ("xp", chunk // 4)], writes=[("s6xpl", slc)],
                inc=kb.dsem(f"s6xpl{slc}"), n=16)
            kb.op("scalar", lambda e, c2=chunk, s9=slc: e.dma_start(
                out=s6_x[s9][:], in_=x_in[c2 * 128:(c2 + 1) * 128, :]),
                reads=[("s6x_f", slc)], writes=[("s6xl", slc)], inc=kb.dsem(f"s6xl{slc}"), n=16)
            for nhf in range(2):
                bb = 4 + nhf
                kb.op("tensor", lambda e, t2=tc, n2=nhf, b2=bb: e.matmul(
                    ps[b2][:], s6_mT[:, t2 * 128:(t2 + 1) * 128],
                    b2e_sb[:, n2 * 512:(n2 + 1) * 512], start=True, stop=True),
                    reads=[("s6mT",), ("b2e",), ("psf", bb)],
                    writes=[("ps", bb)], inc=ct)
            for c in range(DC):
                tb = 6 + c // 4
                last4 = (c % 4 == 3)
                kb.op("tensor", lambda e, t2=tc, c2=c, b2=tb: e.matmul(
                    ps[b2][:, (c2 % 4) * 128:(c2 % 4 + 1) * 128],
                    s6_s[:, c2 * TT6 + t2 * 128:c2 * TT6 + (t2 + 1) * 128],
                    id_sb[:], is_transpose=True),
                    reads=[("s6s",), ("id",), ("psf", tb)],
                    writes=[("ps", tb)] if last4 else [],
                    inc=ct if last4 else None)
            for nhf in range(2):
                kb.op("vector", lambda e, n2=nhf, s9=slc: e.scalar_tensor_tensor(
                    s6_cmb[:, n2 * 512:(n2 + 1) * 512], ps[4 + n2][:], 1.0,
                    s6_yg[s9][:, n2 * 512:(n2 + 1) * 512], ALU.mult, ALU.add),
                    reads=[("ps", 4 + nhf), ("s6yg", slc), ("s6cmb_f",)],
                    writes=[("s6cmbh", nhf), ("psf", 4 + nhf), ("s6yg_f", slc)],
                    inc=cv)
            if chk_c is not None and i == 0 and tc == 0:
                kb.op("gpsimd", lambda e: e.dma_start(out=chk_c[0], in_=s6_cmb[:]),
                      reads=[("s6cmbh", 0), ("s6cmbh", 1)], writes=[("chk0",)], inc=kb.dsem("chk"), n=16)
            kb.op("vector", lambda e, s9=slc: e.tensor_add(
                s6_cmb[:], s6_cmb[:], s6_xp[s9][:]),
                reads=[("s6cmbh", 0), ("s6cmbh", 1), ("s6xpl", slc)]
                      + ([("chk0",)] if chk_c is not None and i == 0 and tc == 0 else []),
                writes=[("s6ffn",), ("s6xp_f", slc)], inc=cv)
            if chk_c is not None and i == 0 and tc == 0:
                kb.op("gpsimd", lambda e: e.dma_start(out=chk_c[1], in_=s6_cmb[:]),
                      reads=[("s6ffn",)], writes=[("chk1",)], inc=kb.dsem("chk"), n=16)
            kb.op("scalar", lambda e: e.activation(s6_g[:], s6_cmb[:], AF.Silu),
                  reads=[("s6ffn",), ("s6g_f",)]
                        + ([("chk1",)] if chk_c is not None and i == 0 and tc == 0 else []),
                  writes=[("s6g",), ("s6cmb_f",)], inc=cs)
            if chk_c is not None and i == 0 and tc == 0:
                kb.op("gpsimd", lambda e: e.dma_start(out=chk_c[2], in_=s6_g[:]),
                      reads=[("s6g",)], writes=[("chk2",)], inc=kb.dsem("chk"), n=16)
            for nhf in range(2):
                kb.op("vector", lambda e, n2=nhf: e.tensor_mul(
                    s6_g[:, n2 * 512:(n2 + 1) * 512], s6_g[:, n2 * 512:(n2 + 1) * 512],
                    ps[6 + n2][:]),
                    reads=[("s6g",), ("ps", 6 + nhf)],
                    writes=[("s6sg", nhf), ("psf", 6 + nhf)], inc=cv)
            if chk_c is not None and i == 0 and tc == 0:
                kb.op("gpsimd", lambda e: e.dma_start(out=chk_c[3], in_=s6_g[:]),
                      reads=[("s6sg", 0), ("s6sg", 1)], writes=[("chk3",)], inc=kb.dsem("chk"), n=16)
            kb.op("vector", lambda e, s9=slc: e.tensor_add(
                s6_out[:], s6_g[:], s6_x[s9][:]),
                reads=[("s6sg", 0), ("s6sg", 1), ("s6xl", slc), ("s6out_f",)]
                      + ([("chk3",)] if chk_c is not None and i == 0 and tc == 0 else []),
                writes=[("s6out",), ("s6g_f",), ("s6x_f", slc)], inc=cv)
            kb.op("gpsimd", lambda e, c2=chunk: e.dma_start(
                out=y_out[c2 * 128:(c2 + 1) * 128, :], in_=s6_out[:]),
                reads=[("s6out",)], writes=[("y", chunk), ("s6out_f",)], inc=kb.dsem("yout"), n=16)
            kb.mark(("s6gi_f", slc), kb.dsem(f"s6yg{slc}"))
        kb.mark(("s6mT_f",), ct)
        kb.mark(("s6s_f",), ct)

    kb.op("gpsimd", lambda e: e.engine_nop(),
          reads=[("y", ch) for ch in range(NT1)])
    kb.flush()
    sD.close()
    sD1.close()
    pers.close()
    ps_all_ctx.__exit__(None, None, None)
    for c in reversed(kb.sem_ctxs):
        c.__exit__(None, None, None)
    blk_ctx.__exit__(None, None, None)
    return nc


_NC_CACHE = {}


def _get_nc(SEQ, T):
    key = (SEQ, T)
    if key not in _NC_CACHE:
        _NC_CACHE[key] = build_program(SEQ, T)
    return _NC_CACHE[key]


def kernel(L0_attn_x, L0_attn_Wz, L0_attn_bz, L0_attn_Wh, L0_attn_bh,
           L0_ffn_ln1_g, L0_ffn_ln1_b, L0_ffn_ln2_g, L0_ffn_ln2_b,
           L0_ffn_wq, L0_ffn_bq, L0_ffn_wk, L0_ffn_bk, L0_ffn_wv, L0_ffn_bv,
           L0_ffn_wo, L0_ffn_bo, L0_ffn_gate_w, L0_ffn_gate_b,
           L0_ffn_w1, L0_ffn_b1, L0_ffn_w2, L0_ffn_b2):
    B, S, Dm = L0_attn_x.shape
    n_cores = 8
    SEQ = B // n_cores
    nc = _get_nc(SEQ, S)
    f = lambda a: np.ascontiguousarray(np.asarray(a, dtype=np.float32))
    shared = {
        "Wz": f(L0_attn_Wz), "bz": f(L0_attn_bz), "Wh": f(L0_attn_Wh), "bh": f(L0_attn_bh),
        "ln1_g": f(L0_ffn_ln1_g), "ln1_b": f(L0_ffn_ln1_b),
        "ln2_g": f(L0_ffn_ln2_g), "ln2_b": f(L0_ffn_ln2_b),
        "wq": f(L0_ffn_wq), "bq": f(L0_ffn_bq), "wk": f(L0_ffn_wk), "bk": f(L0_ffn_bk),
        "wv": f(L0_ffn_wv), "bv": f(L0_ffn_bv), "wo": f(L0_ffn_wo), "bo": f(L0_ffn_bo),
        "gate_w": f(L0_ffn_gate_w), "gate_b": f(L0_ffn_gate_b),
        "w1": f(L0_ffn_w1), "b1": f(L0_ffn_b1), "w2": f(L0_ffn_w2), "b2": f(L0_ffn_b2),
        "ident": np.eye(128, dtype=np.float32),
        "iotac": np.arange(128, dtype=np.float32)[:, None].copy(),
    }
    x = f(L0_attn_x)
    in_maps = []
    for c in range(n_cores):
        m = dict(shared)
        m["x"] = np.ascontiguousarray(x[c * SEQ:(c + 1) * SEQ].reshape(SEQ * S, Dm))
        in_maps.append(m)
    res = run_bass_kernel_spmd(nc, in_maps, core_ids=list(range(n_cores)))
    out = np.concatenate([r["y"].reshape(SEQ, S, Dm) for r in res.results], axis=0)
    return out
